# version 01
# baseline (speedup 1.0000x reference)
import numpy as np
import ml_dtypes
from contextlib import ExitStack
import concourse.bass as bass
import concourse.mybir as mybir
from concourse.bass_utils import run_bass_kernel_spmd

F32 = mybir.dt.float32
BF16 = mybir.dt.bfloat16
I32 = mybir.dt.int32
AF = mybir.ActivationFunctionType
ALU = mybir.AluOpType
AX = mybir.AxisListType
BF = ml_dtypes.bfloat16

NCORE = 8
DM = 2048
TFULL = 4096
TOWN = 2048
NTO = 16
NF = DM // 128
NI = 64
EPS_LN = 1e-5

ENGINES = ("sync", "vector", "scalar", "tensor", "gpsimd")

class Prog:
    def __init__(self, name="prog"):
        self.name = name
        self.nc = bass.Bass("TRN2", target_bir_lowering=False)
        self.code = {e: [] for e in ENGINES}
        self.es = ExitStack()
        self.sems = {}
        self.cnt = {}
        self.t = {}

    def sem(self, name):
        if name not in self.sems:
            self.sems[name] = self.es.enter_context(self.nc.semaphore(name))
            self.cnt[name] = 0
        return self.sems[name]

    def c(self, name):
        self.sem(name)
        return self.cnt[name]

    def sbuf(self, name, shape, dt):
        t = self.es.enter_context(self.nc.sbuf_tensor(name, list(shape), dt))
        self.t[name] = t
        return t

    def psum(self, name, shape, dt=F32):
        t = self.es.enter_context(self.nc.psum_tensor(name, list(shape), dt))
        self.t[name] = t
        return t

    def dram(self, name, shape, dt, kind="Internal", shared=False):
        kw = dict(kind=kind)
        if shared:
            kw["addr_space"] = "Shared"
        t = self.nc.dram_tensor(name, list(shape), dt, **kw)
        self.t[name] = t
        return t

    def emit(self, eng, fn=None, waits=(), inc=None):
        winfo = [(self.sem(s), v) for s, v in waits if v > 0]
        new = None
        iinfo = None
        if inc is not None:
            self.sem(inc[0])
            self.cnt[inc[0]] += inc[1]
            new = self.cnt[inc[0]]
            iinfo = (self.sems[inc[0]], inc[1])
        def run(h, fn=fn, winfo=winfo, iinfo=iinfo):
            for s, v in winfo:
                h.wait_ge(s, v)
            if fn is not None:
                ins = fn(h)
                if iinfo is not None:
                    ins.then_inc(iinfo[0], iinfo[1])
        self.code[eng].append(run)
        return new

    def wait(self, eng, semname, val):
        if val > 0:
            self.emit(eng, None, waits=[(semname, val)])

    def finalize(self):
        block = self.es.enter_context(self.nc.Block())
        for e in ENGINES:
            if not self.code[e]:
                continue
            getattr(block, e)(lambda h, code=tuple(self.code[e]): [fn(h) for fn in code])
        self.es.close()
        return self.nc


def own_cols(p):
    t = np.arange(NTO)
    return ((2 * t + p)[:, None] * 128 + np.arange(128)[None, :]).reshape(-1)

_OWN = [own_cols(0), own_cols(1)]

def x_to_fm(xb, p):
    return np.ascontiguousarray(xb[_OWN[p], :].T.astype(np.float32))

def tile_lhsT(wT):
    K, M = wT.shape
    return np.ascontiguousarray(
        wT.reshape(K // 128, 128, M // 128, 128).transpose(2, 1, 0, 3))

def col128(v):
    return np.ascontiguousarray(v.reshape(-1, 128).T)

def rope_tables_l0():
    d = 128
    inv = 1.0 / 10000.0 ** (np.arange(0, d, 2, dtype=np.float64) / d)
    pos = np.arange(TFULL, dtype=np.float64)
    fr = pos[:, None] * inv[None, :]
    emb = np.concatenate([fr, fr], axis=1)
    return np.cos(emb).astype(np.float32), np.sin(emb).astype(np.float32)

def rope_tables_l1():
    inv = 1.0 / 10000.0 ** (np.arange(0, 64, 2, dtype=np.float64) / 64)
    ang = np.arange(TFULL, dtype=np.float64)[:, None] * inv[None, :]
    return np.cos(ang).astype(np.float32), np.sin(ang).astype(np.float32)



def alloc_consts(P):
    P.sbuf("ones128", [128, 128], F32)
    P.sbuf("ident", [128, 128], BF16)

def load_consts(P, ones_d, ident_d):
    P.emit("sync", lambda h: h.dma_start(out=P.t["ones128"][:], in_=ones_d[:, :]),
           inc=("cL", 16))
    P.emit("sync", lambda h: h.dma_start(out=P.t["ident"][:], in_=ident_d[:, :]),
           inc=("cL", 16))
    return P.cnt["cL"]


def emit_ln(P, tag, x_dram, y_sb, deps=()):
    ones = P.t["ones128"]
    xa = P.sbuf(f"{tag}_xa", [128, 2 * 512], F32)
    xb = P.sbuf(f"{tag}_xb", [128, 2 * 512], F32)
    sq = P.sbuf(f"{tag}_sq", [128, 2 * 512], F32)
    mrs = P.sbuf(f"{tag}_mrs", [128, 4 * 512], F32)
    psA = P.psum(f"{tag}_psA", [128, 512], F32)
    psB = P.psum(f"{tag}_psB", [128, 512], F32)
    m_ = mrs[:, 0:512]; rstd_ = mrs[:, 512:1024]
    tA = mrs[:, 1024:1536]; tB = mrs[:, 1536:2048]
    sL, sL2, sQ, sB, sV, sY = (f"{tag}{x}" for x in ("L", "K", "Q", "B", "V", "Y"))
    for s in (sL, sL2, sQ, sB, sV, sY):
        P.sem(s)
    for tc in range(4):
        ybase = tc * NF
        sb_base = P.cnt[sB]
        sq_prev = P.cnt[sQ]
        sv_prev = P.cnt[sV]
        for f in range(NF):
            buf = f % 2
            sLb = sL + str(buf)
            P.emit("sync", lambda h, f=f, tc=tc, buf=buf: h.dma_start(
                out=xa[:, buf * 512:(buf + 1) * 512],
                in_=x_dram[f * 128:(f + 1) * 128, tc * 512:(tc + 1) * 512]),
                waits=list(deps) + ([(sB, sb_base + f - 1)] if f >= 2 else
                                    ([(sB, sb_base)] if tc > 0 and f < 2 else [])),
                inc=(sLb, 16))
            lw = [(sLb, P.cnt[sLb])]
            sqw = list(lw)
            if f >= 2:
                sqw.append((sB, sb_base + f - 1))
            elif sb_base > 0:
                sqw.append((sB, sb_base))
            cq = P.emit("scalar", lambda h, buf=buf: h.activation(
                out=sq[:, buf * 512:(buf + 1) * 512],
                in_=xa[:, buf * 512:(buf + 1) * 512], func=AF.Square),
                waits=sqw, inc=(sQ, 1))
            P.emit("tensor", lambda h, f=f, buf=buf: h.matmul(
                psA[:], ones[:], xa[:, buf * 512:(buf + 1) * 512],
                start=(f == 0), stop=(f == NF - 1)),
                waits=lw + ([(sQ, sq_prev)] if f == 0 else []))
            P.emit("tensor", lambda h, f=f, buf=buf: h.matmul(
                psB[:], ones[:], sq[:, buf * 512:(buf + 1) * 512],
                start=(f == 0), stop=(f == NF - 1)),
                waits=[(sQ, cq)] + ([(sV, sv_prev)] if f == 0 else []),
                inc=(sB, 1))
        c1 = P.emit("scalar", lambda h: h.activation(
            out=m_, in_=psA[:], func=AF.Copy, scale=1.0 / DM),
            waits=[(sB, sb_base + NF), (sY, ybase)], inc=(sQ, 1))
        c2 = P.emit("scalar", lambda h: h.activation(
            out=tA, in_=m_, func=AF.Square), waits=[(sQ, c1)], inc=(sQ, 1))
        c3 = P.emit("vector", lambda h: h.scalar_tensor_tensor(
            out=tB, in0=psB[:], scalar=1.0 / DM, in1=tA,
            op0=ALU.mult, op1=ALU.subtract), waits=[(sQ, c2)], inc=(sV, 1))
        c3b = P.emit("vector", lambda h: h.tensor_scalar_add(
            out=tB, in0=tB, scalar1=float(EPS_LN)), waits=[(sV, c3)], inc=(sV, 1))
        c4 = P.emit("scalar", lambda h: h.activation(
            out=tA, in_=tB, func=AF.Ln),
            waits=[(sV, c3b)], inc=(sQ, 1))
        c5 = P.emit("scalar", lambda h: h.activation(
            out=rstd_, in_=tA, func=AF.Exp, scale=-0.5),
            waits=[(sQ, c4)], inc=(sQ, 1))
        for f in range(NF):
            buf = f % 2
            sL2b = sL2 + str(buf)
            P.emit("scalar", lambda h, f=f, tc=tc, buf=buf: h.dma_start(
                out=xb[:, buf * 512:(buf + 1) * 512],
                in_=x_dram[f * 128:(f + 1) * 128, tc * 512:(tc + 1) * 512]),
                waits=([(sY, ybase + f - 1)] if f >= 2 else
                       ([(sY, ybase)] if tc > 0 else [])),
                inc=(sL2b, 16))
            l2 = P.cnt[sL2b]
            cs = P.emit("vector", lambda h, buf=buf: h.tensor_sub(
                out=xb[:, buf * 512:(buf + 1) * 512],
                in0=xb[:, buf * 512:(buf + 1) * 512], in1=m_),
                waits=[(sL2b, l2), (sQ, c5)], inc=(sV, 1))
            P.emit("gpsimd", lambda h, f=f, tc=tc, buf=buf: h.tensor_mul(
                out=y_sb[:, f * 2048 + tc * 512: f * 2048 + (tc + 1) * 512],
                in0=xb[:, buf * 512:(buf + 1) * 512], in1=rstd_),
                waits=[(sV, P.cnt[sV])], inc=(sY, 1))
    return (sY, P.cnt[sY])


def emit_ffn(P, tag, y_sb, wg_bf, wu_bf, wd_bf, gsc_sb, usc_sb, dsch_sb,
             x_dram, xo_dram, ydep, wdep=()):
    gu_s = P.sbuf(f"{tag}_gus", [128, 4 * 2048], BF16)
    d_s = P.sbuf(f"{tag}_ds", [128, 2 * 4096], BF16)
    gh = P.sbuf(f"{tag}_gh", [128, 2 * 512], BF16)
    uh = P.sbuf(f"{tag}_uh", [128, 2 * 512], BF16)
    hbuf = P.sbuf(f"{tag}_h", [128, NI * 512], BF16)
    xt = P.sbuf(f"{tag}_xt", [128, 2 * 512], F32)
    xo = P.sbuf(f"{tag}_xo", [128, 2 * 512], F32)
    psG = [P.psum(f"{tag}_psG{i}", [128, 512], F32) for i in range(2)]
    psU = [P.psum(f"{tag}_psU{i}", [128, 512], F32) for i in range(2)]
    psY = [P.psum(f"{tag}_psY{i}", [128, 512], F32) for i in range(2)]
    sW, sD, sGP, sUP, sYP, sGE, sUE, sH, sXL, sYE, sST = (f"{tag}{x}" for x in
        ("W", "D", "g", "u", "y", "G", "U", "H", "X", "E", "S"))
    for s in (sW, sD, sGP, sUP, sYP, sGE, sUE, sH, sXL, sYE, sST):
        P.sem(s)
    NQ = 4
    for tq in range(NQ):
        ystage_prev = tq * NF
        for ic in range(NI):
            buf = ic % 2
            it = tq * NI + ic
            sWb = sW + str(buf)
            P.emit("sync", lambda h, ic=ic, buf=buf: h.dma_start(
                out=gu_s[:, buf * 2048:(buf + 1) * 2048], in_=wg_bf[ic, :, :]),
                waits=([(sGP, it - 1)] if it >= 2 else list(wdep)), inc=(sWb, 16))
            P.emit("sync", lambda h, ic=ic, buf=buf: h.dma_start(
                out=gu_s[:, (2 + buf) * 2048:(3 + buf) * 2048], in_=wu_bf[ic, :, :]),
                waits=([(sUP, it - 1)] if it >= 2 else []), inc=(sWb, 16))
            wready = P.cnt[sWb]
            gw = [(sWb, wready)] + list(ydep)
            if it >= 2:
                gw += [(sGE, it - 1)]
            for f in range(NF):
                P.emit("tensor", lambda h, f=f, buf=buf, tq=tq: h.matmul(
                    psG[buf][:], gu_s[:, buf * 2048 + f * 128: buf * 2048 + (f + 1) * 128],
                    y_sb[:, f * 2048 + tq * 512: f * 2048 + (tq + 1) * 512],
                    start=(f == 0), stop=(f == NF - 1)),
                    waits=(gw if f == 0 else []),
                    inc=((sGP, 1) if f == NF - 1 else None))
            uw = [(sUE, it - 1)] if it >= 2 else []
            for f in range(NF):
                P.emit("tensor", lambda h, f=f, buf=buf, tq=tq: h.matmul(
                    psU[buf][:], gu_s[:, (2 + buf) * 2048 + f * 128: (2 + buf) * 2048 + (f + 1) * 128],
                    y_sb[:, f * 2048 + tq * 512: f * 2048 + (tq + 1) * 512],
                    start=(f == 0), stop=(f == NF - 1)),
                    waits=(uw if f == 0 else []),
                    inc=((sUP, 1) if f == NF - 1 else None))
            gp = P.cnt[sGP]; up = P.cnt[sUP]
            ge = P.emit("scalar", lambda h, ic=ic, buf=buf: h.activation(
                out=gh[:, buf * 512:(buf + 1) * 512], in_=psG[buf][:], func=AF.Silu,
                scale=gsc_sb[:, ic:ic + 1]),
                waits=[(sGP, gp)] + ([(sH, it - 1)] if it >= 2 else []),
                inc=(sGE, 1))
            ue = P.emit("scalar", lambda h, ic=ic, buf=buf: h.activation(
                out=uh[:, buf * 512:(buf + 1) * 512], in_=psU[buf][:], func=AF.Copy,
                scale=usc_sb[:, ic:ic + 1]),
                waits=[(sUP, up)] + ([(sH, it - 1)] if it >= 2 else []),
                inc=(sUE, 1))
            hw = [(sGE, ge), (sUE, ue)]
            if tq > 0 and ic == 0:
                hw.append((sYE, ystage_prev))
            P.emit("gpsimd", lambda h, ic=ic, buf=buf: h.tensor_mul(
                out=hbuf[:, ic * 512:(ic + 1) * 512],
                in0=gh[:, buf * 512:(buf + 1) * 512],
                in1=uh[:, buf * 512:(buf + 1) * 512]),
                waits=hw, inc=(sH, 1))
        h_done = P.cnt[sH]
        for oc in range(NF):
            buf = oc % 2
            it = tq * NF + oc
            P.emit("sync", lambda h, oc=oc: h.dma_start(
                out=d_s[:, 0:4096], in_=wd_bf[oc, :, 0:4096]),
                waits=([(f"{tag}dh", it)] if it >= 1 else list(wdep)), inc=(sD + "0", 16))
            dready = P.cnt[sD + "0"]
            P.emit("sync", lambda h, oc=oc: h.dma_start(
                out=d_s[:, 4096:8192], in_=wd_bf[oc, :, 4096:8192]),
                waits=([(sYP, it)] if it >= 1 else []), inc=(sD + "1", 16))
            dready2 = P.cnt[sD + "1"]
            sXLb = sXL + str(buf)
            P.emit("scalar", lambda h, oc=oc, tq=tq, buf=buf: h.dma_start(
                out=xt[:, buf * 512:(buf + 1) * 512],
                in_=x_dram[oc * 128:(oc + 1) * 128, tq * 512:(tq + 1) * 512]),
                waits=([(sYE, it - 1)] if it >= 2 else []), inc=(sXLb, 16))
            xready = P.cnt[sXLb]
            yw = [(sD + "0", dready), (sH, h_done)]
            if it >= 2:
                yw += [(sYE, it - 1)]
            for ic in range(NI):
                half = ic // 32
                j = ic - half * 32
                if ic == 0:
                    w_ = yw
                elif ic == 32:
                    w_ = [(sD + "1", dready2)]
                else:
                    w_ = []
                if ic == 31:
                    inc_ = (f"{tag}dh", 1)
                elif ic == NI - 1:
                    inc_ = (sYP, 1)
                else:
                    inc_ = None
                P.emit("tensor", lambda h, j=j, half=half, ic=ic, buf=buf: h.matmul(
                    psY[buf][:],
                    d_s[:, half * 4096 + j * 128: half * 4096 + (j + 1) * 128],
                    hbuf[:, ic * 512:(ic + 1) * 512],
                    start=(ic == 0), stop=(ic == NI - 1)),
                    waits=w_, inc=inc_)
            ype = P.cnt[sYP]
            ce = P.emit("vector", lambda h, oc=oc, buf=buf: h.scalar_tensor_tensor(
                out=xo[:, buf * 512:(buf + 1) * 512], in0=psY[buf][:],
                scalar=dsch_sb[:, oc:oc + 1], in1=xt[:, buf * 512:(buf + 1) * 512],
                op0=ALU.mult, op1=ALU.add),
                waits=[(sYP, ype), (sXLb, xready)] +
                      ([(sST + str(buf), (it // 2) * 16)] if it >= 2 else []),
                inc=(sYE, 1))
            P.emit("sync", lambda h, oc=oc, tq=tq, buf=buf: h.dma_start(
                out=xo_dram[oc * 128:(oc + 1) * 128, tq * 512:(tq + 1) * 512],
                in_=xo[:, buf * 512:(buf + 1) * 512]),
                waits=[(sYE, P.cnt[sYE])], inc=(sST + str(buf), 16))
    P.sem(sST + "0"); P.sem(sST + "1")
    return [(sST + "0", P.cnt[sST + "0"]), (sST + "1", P.cnt[sST + "1"])]


def emit_conv_int(P, tag, pairs, deps=()):
    a = P.sbuf(f"{tag}_ci", [128, 2048], I32)
    b = P.sbuf(f"{tag}_cb", [128, 2048], BF16)
    sI, sC, sO = f"{tag}I", f"{tag}C", f"{tag}O"
    for s in (sI, sC, sO):
        P.sem(s)
    it = 0
    for (srct, dstt, nstrip, W) in pairs:
        nk = (W + 2047) // 2048
        for i in range(nstrip):
            for k in range(nk):
                w = min(2048, W - k * 2048)
                P.emit("gpsimd", lambda h, srct=srct, i=i, k=k, w=w: h.dma_start(
                    out=a[:, 0:w],
                    in_=srct[i, :, k * 2048:k * 2048 + w]),
                    waits=list(deps if it == 0 else []) + ([(sC, it)] if it >= 1 else []),
                    inc=(sI, 16))
                li = P.cnt[sI]
                cc = P.emit("vector", lambda h, w=w: h.tensor_scalar_add(
                    out=b[:, 0:w],
                    in0=a[:, 0:w], scalar1=-128.0),
                    waits=[(sI, li)] + ([(sO, it * 16)] if it >= 1 else []),
                    inc=(sC, 1))
                P.emit("gpsimd", lambda h, dstt=dstt, i=i, k=k, w=w: h.dma_start(
                    out=dstt[i, :, k * 2048:k * 2048 + w],
                    in_=b[:, 0:w]),
                    waits=[(sC, cc)], inc=(sO, 16))
                it += 1
    return (sO, P.cnt[sO])

def emit_conv_cast(P, tag, pairs, deps=()):
    s = f"{tag}O"
    P.sem(s)
    for (srct, dstt, nchunk) in pairs:
        rows = srct.shape[0]
        step = rows // nchunk
        for i in range(nchunk):
            P.emit("gpsimd", lambda h, srct=srct, dstt=dstt, i=i, step=step: h.dma_start(
                out=dstt[i * step:(i + 1) * step], in_=srct[i * step:(i + 1) * step]),
                waits=list(deps), inc=(s, 16))
    return (s, P.cnt[s])


def build_ffn_prog(quant, debug=False, ln_only=False):
    P = Prog("ffn0" if quant else "ffn1")
    WDT = I32 if quant else F32
    x = P.dram("x", [DM, TOWN], F32, kind="ExternalInput")
    xo = P.dram("xo", [DM, TOWN], F32, kind="ExternalOutput")
    gw = P.dram("gw", [NI, 128, 2048], WDT, kind="ExternalInput")
    uw = P.dram("uw", [NI, 128, 2048], WDT, kind="ExternalInput")
    dw = P.dram("dw", [NF, 128, 8192], WDT, kind="ExternalInput")
    gsc = P.dram("gsc", [128, NI], F32, kind="ExternalInput")
    usc = P.dram("usc", [128, NI], F32, kind="ExternalInput")
    dsc = P.dram("dsc", [128, NF], F32, kind="ExternalInput")
    ones_d = P.dram("ones_d", [128, 128], F32, kind="ExternalInput")
    wg_bf = P.dram("wg_bf", [NI, 128, 2048], BF16)
    wu_bf = P.dram("wu_bf", [NI, 128, 2048], BF16)
    wd_bf = P.dram("wd_bf", [NF, 128, 8192], BF16)

    P.sbuf("ones128", [128, 128], F32)
    gsc_sb = P.sbuf("gsc_sb", [128, NI], F32)
    usc_sb = P.sbuf("usc_sb", [128, NI], F32)
    dsc_sb = P.sbuf("dsc_sb", [128, NF], F32)
    dsch_sb = P.sbuf("dsch_sb", [128, NF], F32)
    y_sb = P.sbuf("y", [128, NF * 2048], BF16)

    P.emit("sync", lambda h: h.dma_start(out=P.t["ones128"][:], in_=ones_d[:, :]), inc=("cL", 16))
    P.emit("sync", lambda h: h.dma_start(out=gsc_sb[:], in_=gsc[:, :]), inc=("cL", 16))
    P.emit("sync", lambda h: h.dma_start(out=usc_sb[:], in_=usc[:, :]), inc=("cL", 16))
    P.emit("sync", lambda h: h.dma_start(out=dsc_sb[:], in_=dsc[:, :]), inc=("cL", 16))
    cl = P.cnt["cL"]
    ch = P.emit("vector", lambda h: h.tensor_scalar_mul(
        out=dsch_sb[:], in0=dsc_sb[:], scalar1=0.5), waits=[("cL", cl)], inc=("cH", 1))

    if ln_only:
        ydep = emit_ln(P, "ln", x, y_sb, deps=[("cL", cl)])
        ydbg = P.dram("ydbg", [128, NF * 2048], BF16, kind="ExternalOutput")
        mrsdbg = P.dram("mrsdbg", [128, 4 * 512], F32, kind="ExternalOutput")
        P.emit("sync", lambda h: h.dma_start(out=ydbg[:, :], in_=y_sb[:]),
               waits=[ydep], inc=("dS", 16))
        P.emit("sync", lambda h: h.dma_start(out=mrsdbg[:, :], in_=P.t["ln_mrs"][:]),
               waits=[ydep], inc=("dS", 16))
        P.wait("sync", "dS", 32)
        P.finalize()
        return P
    if quant:
        cdep = emit_conv_int(P, "cv", [(gw, wg_bf, NI, 2048), (uw, wu_bf, NI, 2048),
                                       (dw, wd_bf, NF, 8192)])
    else:
        cdep = emit_conv_cast(P, "cv", [(gw, wg_bf, 16), (uw, wu_bf, 16), (dw, wd_bf, 16)])

    ydep = emit_ln(P, "ln", x, y_sb, deps=[("cL", cl)])
    st = emit_ffn(P, "ff", y_sb, wg_bf, wu_bf, wd_bf, gsc_sb, usc_sb, dsch_sb,
                  x, xo, ydep=[ydep, ("cH", ch)], wdep=[cdep])
    for s_, v_ in st:
        P.wait("sync", s_, v_)
    if debug:
        ydbg = P.dram("ydbg", [128, NF * 2048], BF16, kind="ExternalOutput")
        hdbg = P.dram("hdbg", [128, NI * 512], BF16, kind="ExternalOutput")
        P.emit("sync", lambda h: h.dma_start(out=ydbg[:, :], in_=y_sb[:]), inc=("dS", 16))
        P.emit("sync", lambda h: h.dma_start(out=hdbg[:, :], in_=P.t["ff_h"][:]), inc=("dS", 16))
        P.wait("sync", "dS", 32)
    P.finalize()
    return P


def prep_ffn_inputs(inp, layer):
    ones = np.ones((128, 128), np.float32)
    if layer == 0:
        gw = tile_lhsT(np.ascontiguousarray(inp["L0_ffn_gate_w"].T)).reshape(NI, 128, 2048)
        uw = tile_lhsT(np.ascontiguousarray(inp["L0_ffn_up_w"].T)).reshape(NI, 128, 2048)
        dw = tile_lhsT(np.ascontiguousarray(inp["L0_ffn_down_w"].T)).reshape(NF, 128, 8192)
        gsc = col128(inp["L0_ffn_gate_scale"])
        usc = col128(inp["L0_ffn_up_scale"])
        dsc = col128(inp["L0_ffn_down_scale"])
    else:
        gw = tile_lhsT(np.ascontiguousarray(inp["L1_ffn_w_gate"].T)).reshape(NI, 128, 2048)
        uw = tile_lhsT(np.ascontiguousarray(inp["L1_ffn_w_up"].T)).reshape(NI, 128, 2048)
        dw = tile_lhsT(np.ascontiguousarray(inp["L1_ffn_w_down"].T)).reshape(NF, 128, 8192)
        ones_v = np.ones(8192, np.float32)
        gsc = col128(ones_v)
        usc = col128(ones_v)
        dsc = col128(np.ones(2048, np.float32))
    return dict(gw=gw, uw=uw, dw=dw, gsc=gsc, usc=usc, dsc=dsc, ones_d=ones)


def emit_proj_kind(P, tag, y_sb, strips, dst, rope, tm, cosT, sinT, deps,
                   nout=NF, ntc=4, rms=None, src_nf=NF, to_sbuf=None):
    ident = P.t["ident"]
    psP = [P.t["at_psP0"], P.t["at_psP1"]]
    psCs = [P.t["at_psC0"], P.t["at_psC1"]]
    psS = P.t["at_psS"]
    w_s = P.t["at_ws"]
    qs = P.t["at_qs"]
    tmpc = P.t["at_tc"]
    tmps = P.t["at_ts2"]
    rr = P.t["at_rr"]
    ob = P.t["at_ob"]
    tt = P.t["at_tt"]
    ones = P.t["ones128"]
    sW, sP, sE, sR, sT, sC, sS = (f"{tag}{x}" for x in "WPERTCS")
    for s in (sW, sP, sE, sR, sT, sC, sS):
        P.sem(s)
    rinc = 2 if (rope and not rms) else 1
    for o in range(nout):
        buf = o % 2
        sWb = sW + str(buf)
        sw_w = src_nf * 128
        P.emit("sync", lambda h, o=o, buf=buf, sw_w=sw_w: h.dma_start(
            out=w_s[:, buf * 2048:buf * 2048 + sw_w], in_=strips[o, :, 0:sw_w]),
            waits=(list(deps) if o < 2 else [(sP, (o - 1) * ntc)]), inc=(sWb, 16))
        wready = P.cnt[sWb]
        for tc in range(ntc):
            it = o * ntc + tc
            pbuf = it % 2
            mw = [(sWb, wready)]
            if it >= 2:
                mw.append((sE, it - 1))
            for f in range(src_nf):
                P.emit("tensor", lambda h, f=f, tc=tc, buf=buf, pbuf=pbuf: h.matmul(
                    psP[pbuf][:], w_s[:, buf * 2048 + f * 128: buf * 2048 + (f + 1) * 128],
                    y_sb[:, f * 2048 + tc * 512: f * 2048 + (tc + 1) * 512],
                    start=(f == 0), stop=(f == src_nf - 1)),
                    waits=(mw if f == 0 else []),
                    inc=((sP, 1) if f == src_nf - 1 else None))
            pp = P.cnt[sP]
            buf2 = it % 2
            q_ = qs[:, buf2 * 512:(buf2 + 1) * 512]
            ew = [(sP, pp)]
            if it >= 2:
                if to_sbuf is not None:
                    ew.append((sS + str(it % 2), 16 * (it // 2)))
                else:
                    ew.append((sR, rinc * (it - 1)))
            ev = P.emit("scalar", lambda h, pbuf=pbuf, q_=q_: h.activation(
                out=q_, in_=psP[pbuf][:], func=AF.Copy), waits=ew, inc=(sE, 1))
            ob_ = ob[:, buf2 * 512:(buf2 + 1) * 512]
            obw = [(sT, 4 * (it - 1))] if (tm and it >= 2) else \
                  ([(sS + str(buf2), 16 * (it // 2))] if (not tm and it >= 2) else [])
            if rope and not rms:
                cs = cosT[:, tc * 512:(tc + 1) * 512]
                sn = sinT[:, tc * 512:(tc + 1) * 512]
                P.emit("vector", lambda h, q_=q_, cs=cs: h.tensor_mul(
                    out=tmpc[:], in0=q_, in1=cs),
                    waits=[(sE, P.cnt[sE]), (sR, rinc * it)], inc=None)
                P.emit("vector", lambda h, q_=q_, sn=sn: h.tensor_mul(
                    out=tmps[0:64, :], in0=q_[64:128, :], in1=sn[64:128, :]), inc=None)
                vv = P.emit("vector", lambda h, q_=q_, sn=sn: h.tensor_mul(
                    out=tmps[64:128, :], in0=q_[0:64, :], in1=sn[0:64, :]),
                    inc=(f"{tag}V", 1))
                P.emit("gpsimd", lambda h, ob_=ob_: h.tensor_sub(
                    out=ob_[0:64, :], in0=tmpc[0:64, :], in1=tmps[0:64, :]),
                    waits=[(f"{tag}V", vv)] + obw, inc=(sR, 1))
                P.emit("gpsimd", lambda h, ob_=ob_: h.tensor_add(
                    out=ob_[64:128, :], in0=tmpc[64:128, :], in1=tmps[64:128, :]),
                    inc=(sR, 1))
            elif rms is not None:
                wv_ap, eps, p32 = rms
                cs = cosT[:, tc * 512:(tc + 1) * 512] if rope else None
                sn = sinT[:, tc * 512:(tc + 1) * 512] if rope else None
                P.emit("vector", lambda h, q_=q_: h.tensor_copy(
                    out=tmpc[64:128, :], in_=q_[64:128, :]),
                    waits=[(sE, P.cnt[sE]), (sR, it)], inc=None)
                if rope:
                    P.emit("vector", lambda h, q_=q_, sn=sn: h.tensor_mul(
                        out=tmps[0:32, :], in0=q_[32:64, :], in1=sn[32:64, :]), inc=None)
                    P.emit("vector", lambda h, q_=q_, cs=cs: h.tensor_mul(
                        out=tmpc[0:32, :], in0=q_[0:32, :], in1=cs[0:32, :]), inc=None)
                    P.emit("vector", lambda h, q_=q_, sn=sn: h.tensor_mul(
                        out=tmps[32:64, :], in0=q_[0:32, :], in1=sn[0:32, :]), inc=None)
                    vv = P.emit("vector", lambda h, q_=q_, cs=cs: h.tensor_mul(
                        out=tmpc[32:64, :], in0=q_[32:64, :], in1=cs[32:64, :]),
                        inc=(f"{tag}V", 1))
                    P.emit("gpsimd", lambda h: h.tensor_sub(
                        out=tmpc[0:32, :], in0=tmpc[0:32, :], in1=tmps[0:32, :]),
                        waits=[(f"{tag}V", vv)], inc=(f"{tag}V", 1))
                    P.emit("gpsimd", lambda h: h.tensor_add(
                        out=tmpc[32:64, :], in0=tmpc[32:64, :], in1=tmps[32:64, :]),
                        waits=[(f"{tag}V", P.cnt[f"{tag}V"])], inc=(f"{tag}V", 1))
                    vv2 = P.cnt[f"{tag}V"]
                else:
                    vv2 = P.emit("gpsimd", lambda h: h.engine_nop(), inc=(f"{tag}V", 1))
                sq2 = P.emit("scalar", lambda h: h.activation(
                    out=tmps[:], in_=tmpc[:], func=AF.Square),
                    waits=[(f"{tag}V", vv2)], inc=(f"{tag}Q", 1))
                mmst = P.emit("tensor", lambda h: h.matmul(
                    psS[:], ones[:], tmps[:], start=True, stop=True),
                    waits=[(f"{tag}Q", sq2)], inc=(f"{tag}M", 1))
                a1 = P.emit("vector", lambda h: h.scalar_tensor_tensor(
                    out=rr[:], in0=psS[:], scalar=1.0 / 128.0, in1=P.t["epsb"][:, 0:512],
                    op0=ALU.mult, op1=ALU.add),
                    waits=[(f"{tag}M", mmst)], inc=(f"{tag}Q", 1))
                a2 = P.emit("scalar", lambda h: h.activation(
                    out=rr[:], in_=rr[:], func=AF.Ln), waits=[(f"{tag}Q", P.cnt[f"{tag}Q"])],
                    inc=(f"{tag}Q", 1))
                a3 = P.emit("scalar", lambda h: h.activation(
                    out=rr[:], in_=rr[:], func=AF.Exp, scale=-0.5),
                    waits=[(f"{tag}Q", P.cnt[f"{tag}Q"])], inc=(f"{tag}Q", 1))
                P.emit("vector", lambda h, ob_=ob_, wv_ap=wv_ap: h.scalar_tensor_tensor(
                    out=ob_, in0=tmpc[:], scalar=wv_ap, in1=rr[:],
                    op0=ALU.mult, op1=ALU.mult),
                    waits=[(f"{tag}Q", P.cnt[f"{tag}Q"])] + obw, inc=(sR, 1))
            elif to_sbuf is not None:
                P.emit("vector", lambda h, o=o, tc=tc, q_=q_: h.tensor_copy(
                    out=to_sbuf[:, o * 2048 + tc * 512:o * 2048 + (tc + 1) * 512],
                    in_=q_),
                    waits=[(sE, P.cnt[sE])], inc=(sS + str(buf2), 16))
                continue
            else:
                P.emit("vector", lambda h, q_=q_, ob_=ob_: h.tensor_copy(
                    out=ob_, in_=q_),
                    waits=[(sE, P.cnt[sE]), (sR, it)] + obw, inc=(sR, 1))
            rdone = P.cnt[sR]
            if not tm:
                P.emit("sync", lambda h, o=o, tc=tc, ob_=ob_: h.dma_start(
                    out=dst[o * 128:(o + 1) * 128, tc * 512:(tc + 1) * 512], in_=ob_),
                    waits=[(sR, rdone)], inc=(sS + str(buf2), 16))
            else:
                half = it % 2
                for j in range(4):
                    P.emit("tensor", lambda h, j=j, half=half, ob_=ob_: h.transpose(
                        psCs[half][:, j * 128:(j + 1) * 128],
                        ob_[:, j * 128:(j + 1) * 128], ident[:]),
                        waits=([(sR, rdone), (sC, max(0, it - 1))] if j == 0 else []),
                        inc=((sT, 4) if j == 3 else None))
                tdone = P.cnt[sT]
                cw = [(sT, tdone)]
                if it >= 2:
                    cw.append((sS + str(buf2), 16 * (it // 2)))
                P.emit("scalar", lambda h, half=half, buf2=buf2: h.activation(
                    out=tt[:, buf2 * 512:(buf2 + 1) * 512],
                    in_=psCs[half][:, 0:512], func=AF.Copy),
                    waits=cw, inc=(sC, 1))
                P.emit("sync", lambda h, o=o, tc=tc, buf2=buf2: h.dma_start(
                    out=dst[o, :, tc * 512:(tc + 1) * 512],
                    in_=tt[:, buf2 * 512:(buf2 + 1) * 512]),
                    waits=[(sC, P.cnt[sC])], inc=(sS + str(buf2), 16))
    P.sem(sS + "0"); P.sem(sS + "1")
    return [(sS + "0", P.cnt[sS + "0"]), (sS + "1", P.cnt[sS + "1"])]


def alloc_attn_common(P):
    P.sbuf("at_ws", [128, 2 * 2048], BF16)
    P.sbuf("at_qs", [128, 2 * 512], F32)
    P.sbuf("at_tc", [128, 512], F32)
    P.sbuf("at_ts2", [128, 512], F32)
    P.sbuf("at_rr", [128, 512], F32)
    P.sbuf("at_ob", [128, 2 * 512], BF16)
    P.sbuf("at_tt", [128, 2 * 512], BF16)
    P.sbuf("epsb", [128, 512], F32)
    P.psum("at_psP0", [128, 512], F32)
    P.psum("at_psP1", [128, 512], F32)
    P.psum("at_psC0", [128, 512], BF16)
    P.psum("at_psC1", [128, 512], BF16)
    P.psum("at_psS", [128, 512], F32)

def emit_fill_epsb(P, val):
    return P.emit("vector", lambda h: h.memset(P.t["epsb"][:], float(val)),
                  inc=("epsF", 1))


def emit_l0_pass1(P, q_d, kxg_d, Mt_sb, ts_d, par, deps):
    ident = P.t["ident"]
    ones = P.t["ones128"]
    kf = P.sbuf("p1_kf", [128, 4096], BF16)
    kt = P.sbuf("p1_kt", [128, 4096], BF16)
    qh = P.sbuf("p1_qh", [128, 2048], BF16)
    Eb = P.sbuf("p1_E", [128, 4096], F32)
    acc = P.sbuf("p1_acc", [128, 4096], F32)
    zp = P.sbuf("p1_zp", [128, 8], F32)
    zs = P.sbuf("p1_zs", [128, 2], F32)
    tsr = P.sbuf("p1_ts", [128, 512], F32)
    tso = P.sbuf("p1_tso", [1, 4096], F32)
    psK = P.t["at_psC"]
    psA = P.t["p1_psA"]
    psB = P.t["p1_psB"]
    psF = P.t["at_psP0"]
    SCALE = float(1.0 / np.sqrt(128.0))
    sL, sK, sT, sQm, sE, sZ, sA, sF, sS = (f"p1{x}" for x in "LKTQEZAFS")
    for s in (sL, sK, sT, sQm, sE, sZ, sA, sF, sS):
        P.sem(s)
    for hh in range(16):
        P.emit("sync", lambda h, hh=hh: h.dma_start(out=kt[:], in_=kxg_d[hh, :, :]),
               waits=list(deps) + ([(sT, 32 * hh)] if hh > 0 else []), inc=(sL, 16))
        P.emit("sync", lambda h, hh=hh: h.dma_start(
            out=qh[:], in_=q_d[hh * 128:(hh + 1) * 128, :]),
            waits=([(sA, hh * 16)] if hh > 0 else []), inc=(sL, 16))
        lds = P.cnt[sL]
        for g in range(32):
            half = g % 2
            tw = [(sL, lds)] if g == 0 else []
            if g >= 2:
                tw.append((sK, 32 * hh + g - 1))
            P.emit("tensor", lambda h, g=g, half=half: h.transpose(
                psK[:, half * 512:half * 512 + 128], kt[:, g * 128:(g + 1) * 128],
                ident[:]), waits=tw, inc=(sT, 1))
            P.emit("vector", lambda h, g=g, half=half: h.tensor_copy(
                out=kf[:, g * 128:(g + 1) * 128], in_=psK[:, half * 512:half * 512 + 128]),
                waits=[(sT, P.cnt[sT])], inc=(sK, 1))
        kdone = P.cnt[sK]
        am = P.emit("gpsimd", lambda h: h.memset(acc[:], 0.0),
                    waits=([(sF, hh * 8)] if hh > 0 else []), inc=(f"p1G", 1))
        for t in range(NTO):
            W = (2 * t + 2) * 128
            nch = (W + 1023) // 1024
            zparts = []
            for c in range(nch):
                w0 = c * 1024
                w1 = min(W, w0 + 1024)
                ps = psA if c % 2 == 0 else psB
                for s2 in range(w0, w1, 512):
                    sw = min(512, w1 - s2)
                    mw = []
                    if s2 == w0:
                        mw = [(sK, kdone), (sL, lds)]
                        if hh > 0 or t > 0 or c >= 2:
                            mw.append((sE, max(0, P.cnt[sQm] - 1)))
                    P.emit("tensor", lambda h, t=t, s2=s2, sw=sw, ps=ps, w0=w0: h.matmul(
                        ps[:, s2 - w0:s2 - w0 + sw], qh[:, t * 128:(t + 1) * 128],
                        kf[:, s2:s2 + sw], start=True, stop=True),
                        waits=mw, inc=None)
                P.emit("tensor", lambda h: h.nop(), inc=(sQm, 1))
                qm = P.cnt[sQm]
                dstart = W - 256 - w0
                main_hi = min(w1, W - 256) - w0
                if main_hi > 0:
                    zcol = len(zparts)
                    ce = P.emit("scalar", lambda h, w0=w0, main_hi=main_hi, ps=ps, zcol=zcol:
                        h.activation(out=Eb[:, w0:w0 + main_hi], in_=ps[:, 0:main_hi],
                                     func=AF.Exp, scale=SCALE,
                                     accum_out=zp[:, zcol:zcol + 1]),
                        waits=[(sQm, qm)], inc=(sE, 1))
                    zparts.append(("zp", zcol))
                if w1 > W - 256:
                    d0 = max(0, dstart)
                    ce2 = P.emit("scalar", lambda h, w0=w0, d0=d0, w1=w1, ps=ps:
                        h.activation(out=Eb[:, w0 + d0:w1], in_=ps[:, d0:w1 - w0],
                                     func=AF.Exp, scale=SCALE),
                        waits=[(sQm, qm)], inc=(sE, 1))
            ee = P.cnt[sE]
            dm = P.emit("vector", lambda h, t=t, W=W: h.scalar_tensor_tensor(
                out=Eb[:, W - 256:W], in0=Eb[:, W - 256:W], scalar=1.0,
                in1=Mt_sb[:, t * 256:(t + 1) * 256], op0=ALU.mult, op1=ALU.mult,
                accum_out=zs[:, 0:1]),
                waits=[(sE, ee)], inc=(sZ, 1))
            nz = len(zparts)
            zv = P.emit("vector", lambda h, nz=nz: h.reduce_sum(
                out=zs[:, 1:2], in_=zp[:, 0:nz], axis=AX.X) if nz > 0 else
                h.tensor_copy(out=zs[:, 1:2], in_=zs[:, 0:1]),
                waits=[(sZ, P.cnt[sZ])], inc=(sZ, 1))
            if nz > 0:
                zv = P.emit("vector", lambda h: h.tensor_add(
                    out=zs[:, 1:2], in0=zs[:, 1:2], in1=zs[:, 0:1]),
                    waits=[(sZ, zv)], inc=(sZ, 1))
            wv = P.emit("vector", lambda h: h.reciprocal(
                out=zs[:, 0:1], in_=zs[:, 1:2]), waits=[(sZ, P.cnt[sZ])], inc=(sZ, 1))
            aa = P.emit("vector", lambda h, W=W: h.scalar_tensor_tensor(
                out=acc[:, 0:W], in0=Eb[:, 0:W], scalar=zs[:, 0:1], in1=acc[:, 0:W],
                op0=ALU.mult, op1=ALU.add),
                waits=[(sZ, P.cnt[sZ]), ("p1G", P.cnt["p1G"])], inc=(sA, 1))
            P.emit("vector", None, waits=[])
        P.wait("vector", sA, P.cnt[sA])
        accf = P.cnt[sA]
        for c8 in range(8):
            fm = P.emit("tensor", lambda h, c8=c8: h.matmul(
                psF[:], ones[:], acc[:, c8 * 512:(c8 + 1) * 512], start=True, stop=True),
                waits=[(sA, accf)] + ([(sF, hh * 8 + c8)] if (hh + c8) > 0 else []),
                inc=(sQm, 1))
            P.emit("vector", lambda h, c8=c8: h.tensor_copy(
                out=tso[0:1, c8 * 512:(c8 + 1) * 512], in_=psF[0:1, :]),
                waits=[(sQm, P.cnt[sQm])], inc=(sF, 1))
        P.emit("sync", lambda h, hh=hh: h.dma_start(
            out=ts_d[hh:hh + 1, :], in_=tso[0:1, :]),
            waits=[(sF, P.cnt[sF])], inc=(sS, 16))
    return (sS, P.cnt[sS])


def emit_l0_pass1(P, q_d, kxg_d, Mt_sb, ts_d, deps):
    ident = P.t["ident"]
    ones = P.t["ones128"]
    kf = P.sbuf("p1_kf", [128, 4096], BF16)
    kt = P.sbuf("p1_kt", [128, 4096], BF16)
    qh = P.sbuf("p1_qh", [128, 2048], BF16)
    Eb = P.sbuf("p1_E", [128, 4096], F32)
    acc = P.sbuf("p1_acc", [128, 4096], F32)
    zp = P.sbuf("p1_zp", [128, 8], F32)
    zs = P.sbuf("p1_zs", [128, 2], F32)
    tso = P.sbuf("p1_tso", [1, 4096], F32)
    psKs = [P.t["at_psC0"], P.t["at_psC1"]]
    psA = P.t["p1_psA"]
    psB = P.t["p1_psB"]
    psF = P.t["at_psP0"]
    SCALE = float(1.0 / np.sqrt(128.0))
    sL, sK, sT, sQm, sE, sZ, sA, sF, sS, sG = (f"p1{x}" for x in "LKTQEZAFSG")
    for s in (sL, sK, sT, sQm, sE, sZ, sA, sF, sS, sG):
        P.sem(s)
    gch = 0
    for hh in range(16):
        qm_head = P.cnt[sQm]
        P.emit("sync", lambda h, hh=hh: h.dma_start(out=kt[:], in_=kxg_d[hh, :, :]),
               waits=list(deps) + ([(sT, 32 * hh)] if hh > 0 else []), inc=(sL, 16))
        P.emit("sync", lambda h, hh=hh: h.dma_start(
            out=qh[:], in_=q_d[hh * 128:(hh + 1) * 128, :]),
            waits=([(sQm, qm_head)] if hh > 0 else []), inc=(sL, 16))
        lds = P.cnt[sL]
        for g in range(32):
            half = g % 2
            tw = [(sL, lds)] if g == 0 else []
            if g >= 2:
                tw.append((sK, 32 * hh + g - 1))
            P.emit("tensor", lambda h, g=g, half=half: h.transpose(
                psKs[half][:, 0:128], kt[:, g * 128:(g + 1) * 128],
                ident[:]), waits=tw, inc=(sT, 1))
            kw = [(sT, P.cnt[sT])]
            if g == 0 and hh > 0:
                kw.append((sQm, qm_head))
            P.emit("vector", lambda h, g=g, half=half: h.tensor_copy(
                out=kf[:, g * 128:(g + 1) * 128],
                in_=psKs[half][:, 0:128]),
                waits=kw, inc=(sK, 1))
        kdone = P.cnt[sK]
        P.emit("gpsimd", lambda h: h.memset(acc[:], 0.0),
               waits=([(sF, hh * 8)] if hh > 0 else []), inc=(sG, 1))
        accg = P.cnt[sG]
        for t in range(NTO):
            W = (2 * t + 2) * 128
            nch = (W + 1023) // 1024
            a_tile = P.cnt[sA]
            nz = 0
            for c in range(nch):
                w0 = c * 1024
                w1 = min(W, w0 + 1024)
                ps = psA if gch % 2 == 0 else psB
                nsub = (w1 - w0 + 511) // 512
                for si in range(nsub):
                    s2 = w0 + si * 512
                    sw = min(512, w1 - s2)
                    mw = []
                    if si == 0:
                        mw = [(sK, kdone), (sL, lds)]
                        if gch >= 2:
                            mw.append((sE, gch - 1))
                    P.emit("tensor", lambda h, t=t, s2=s2, sw=sw, ps=ps, w0=w0: h.matmul(
                        ps[:, s2 - w0:s2 - w0 + sw], qh[:, t * 128:(t + 1) * 128],
                        kf[:, s2:s2 + sw], start=True, stop=True),
                        waits=mw, inc=((sQm, 1) if si == nsub - 1 else None))
                qm = P.cnt[sQm]
                main_hi = min(w1, W - 256) - w0
                acts = []
                if main_hi > 0:
                    acts.append(("main", 0, main_hi, nz)); nz += 1
                if w1 > W - 256:
                    d0 = max(0, W - 256 - w0)
                    acts.append(("diag", d0, w1 - w0, -1))
                for idx, (knd, a0, a1, zcol) in enumerate(acts):
                    last = idx == len(acts) - 1
                    aw = [(sQm, qm)]
                    if c == 0 and idx == 0:
                        aw.append((sA, a_tile))
                    if knd == "main":
                        P.emit("scalar", lambda h, w0=w0, a0=a0, a1=a1, ps=ps, zcol=zcol:
                            h.activation(out=Eb[:, w0 + a0:w0 + a1], in_=ps[:, a0:a1],
                                         func=AF.Exp, scale=SCALE,
                                         accum_out=zp[:, zcol:zcol + 1]),
                            waits=aw, inc=((sE, 1) if last else None))
                    else:
                        P.emit("scalar", lambda h, w0=w0, a0=a0, a1=a1, ps=ps:
                            h.activation(out=Eb[:, w0 + a0:w0 + a1], in_=ps[:, a0:a1],
                                         func=AF.Exp, scale=SCALE),
                            waits=aw, inc=((sE, 1) if last else None))
                gch += 1
            ee = P.cnt[sE]
            P.emit("vector", lambda h, t=t, W=W: h.scalar_tensor_tensor(
                out=Eb[:, W - 256:W], in0=Eb[:, W - 256:W], scalar=1.0,
                in1=Mt_sb[:, t * 256:(t + 1) * 256], op0=ALU.mult, op1=ALU.mult,
                accum_out=zs[:, 0:1]),
                waits=[(sE, ee)], inc=(sZ, 1))
            if nz > 0:
                P.emit("vector", lambda h, nz=nz: h.reduce_sum(
                    out=zs[:, 1:2], in_=zp[:, 0:nz], axis=AX.X),
                    waits=[(sZ, P.cnt[sZ])], inc=(sZ, 1))
                P.emit("vector", lambda h: h.tensor_add(
                    out=zs[:, 1:2], in0=zs[:, 1:2], in1=zs[:, 0:1]),
                    waits=[(sZ, P.cnt[sZ])], inc=(sZ, 1))
            else:
                P.emit("vector", lambda h: h.tensor_copy(
                    out=zs[:, 1:2], in_=zs[:, 0:1]),
                    waits=[(sZ, P.cnt[sZ])], inc=(sZ, 1))
            P.emit("vector", lambda h: h.reciprocal(
                out=zs[:, 0:1], in_=zs[:, 1:2]), waits=[(sZ, P.cnt[sZ])], inc=(sZ, 1))
            P.emit("vector", lambda h, W=W: h.scalar_tensor_tensor(
                out=acc[:, 0:W], in0=Eb[:, 0:W], scalar=zs[:, 0:1], in1=acc[:, 0:W],
                op0=ALU.mult, op1=ALU.add),
                waits=[(sZ, P.cnt[sZ]), (sG, accg)], inc=(sA, 1))
        accf = P.cnt[sA]
        for c8 in range(8):
            P.emit("tensor", lambda h, c8=c8: h.matmul(
                psF[:], ones[:], acc[:, c8 * 512:(c8 + 1) * 512], start=True, stop=True),
                waits=[(sA, accf)] + ([(sF, hh * 8 + c8)] if (hh + c8) > 0 else []),
                inc=(sQm, 1))
            P.emit("vector", lambda h, c8=c8: h.tensor_copy(
                out=tso[0:1, c8 * 512:(c8 + 1) * 512], in_=psF[0:1, :]),
                waits=[(sQm, P.cnt[sQm])] + ([(sS, hh * 16)] if (hh > 0 and c8 == 0) else []),
                inc=(sF, 1))
        P.emit("sync", lambda h, hh=hh: h.dma_start(
            out=ts_d[hh:hh + 1, :], in_=tso[0:1, :]),
            waits=[(sF, P.cnt[sF])], inc=(sS, 16))
    return (sS, P.cnt[sS])


def emit_bisect(P, ts_self, ts_part, hm_d, deps):
    tsb = P.sbuf("bi_ts", [128, 2 * 512], F32)
    lohi = P.sbuf("bi_lh", [128, 4], F32)
    msk = P.sbuf("bi_msk", [128, 1], I32)
    ge = P.sbuf("bi_ge", [128, 512], F32)
    cnt = P.sbuf("bi_cnt", [128, 2], F32)
    blk8 = P.t["blk8_sb"]; bc16 = P.t["bc16_sb"]
    psP0 = P.t["at_psP0"]; psP1 = P.t["at_psP1"]
    sL, sV, sM = "biL", "biV", "biM"
    for s in (sL, sV, sM):
        P.sem(s)
    P.emit("sync", lambda h: h.dma_start(
        out=tsb[:, 0:512], in_=ts_self.rearrange("h (s c) -> (h s) c", s=8)),
        waits=list(deps), inc=(sL, 16))
    P.emit("sync", lambda h: h.dma_start(
        out=tsb[:, 512:1024], in_=ts_part.rearrange("h (s c) -> (h s) c", s=8)),
        waits=list(deps), inc=(sL, 16))
    v0 = P.emit("vector", lambda h: h.tensor_add(
        out=tsb[:, 0:512], in0=tsb[:, 0:512], in1=tsb[:, 512:1024]),
        waits=[(sL, 32)], inc=(sV, 1))
    P.emit("vector", lambda h: h.memset(lohi[:, 0:1], 0.0), inc=(sV, 1))
    P.emit("vector", lambda h: h.memset(lohi[:, 1:2], 4096.0), inc=(sV, 1))
    for it in range(34):
        v1 = P.emit("vector", lambda h: h.tensor_add(
            out=lohi[:, 2:3], in0=lohi[:, 0:1], in1=lohi[:, 1:2]),
            waits=[(sV, P.cnt[sV])], inc=(sV, 1))
        v2 = P.emit("vector", lambda h: h.tensor_scalar_mul(
            out=lohi[:, 2:3], in0=lohi[:, 2:3], scalar1=0.5),
            waits=[(sV, v1)], inc=(sV, 1))
        v3 = P.emit("vector", lambda h: h.tensor_scalar(
            out=ge[:], in0=tsb[:, 0:512], scalar1=lohi[:, 2:3], scalar2=None,
            op0=ALU.is_ge), waits=[(sV, v2)], inc=(sV, 1))
        v4 = P.emit("vector", lambda h: h.reduce_sum(
            out=cnt[:, 0:1], in_=ge[:], axis=AX.X), waits=[(sV, v3)], inc=(sV, 1))
        m1 = P.emit("tensor", lambda h: h.matmul(
            psP0[0:16, 0:1], blk8[:], cnt[:, 0:1], start=True, stop=True),
            waits=[(sV, v4)] + ([(sM, P.cnt[sM])] if it > 0 else []), inc=(sM, 1))
        v5 = P.emit("vector", lambda h: h.tensor_scalar(
            out=cnt[0:16, 1:2], in0=psP0[0:16, 0:1], scalar1=408.5, scalar2=None,
            op0=ALU.is_ge), waits=[(sM, m1)], inc=(sV, 1))
        m2 = P.emit("tensor", lambda h: h.matmul(
            psP1[:, 0:1], bc16[:], cnt[0:16, 1:2], start=True, stop=True),
            waits=[(sV, v5)], inc=(sM, 1))
        v6 = P.emit("vector", lambda h: h.tensor_copy(
            out=msk[:], in_=psP1[:, 0:1]), waits=[(sM, P.cnt[sM])], inc=(sV, 1))
        v7 = P.emit("vector", lambda h: h.select(
            out=lohi[:, 0:1], mask=msk[:], on_true=lohi[:, 2:3],
            on_false=lohi[:, 0:1]), waits=[(sV, v6)], inc=(sV, 1))
        P.emit("vector", lambda h: h.select(
            out=lohi[:, 1:2], mask=msk[:], on_true=lohi[:, 1:2],
            on_false=lohi[:, 2:3]), waits=[(sV, v7)], inc=(sV, 1))
    vh = P.emit("vector", lambda h: h.tensor_scalar(
        out=ge[:], in0=tsb[:, 0:512], scalar1=lohi[:, 0:1], scalar2=None,
        op0=ALU.is_ge), waits=[(sV, P.cnt[sV])], inc=(sV, 1))
    st = P.emit("sync", lambda h: h.dma_start(
        out=hm_d.rearrange("h (s c) -> (h s) c", s=8), in_=ge[:]),
        waits=[(sV, vh)], inc=(sL, 16))
    return (sL, P.cnt[sL])


def emit_l0_pass2(P, q_d, kxg_d, vxg_d, hm_d, Tb_sb, Mt_sb, ao_all, deps):
    ident = P.t["ident"]
    kf = P.sbuf("p2_kf", [128, 4096], BF16)
    kt = P.sbuf("p2_kt", [128, 4096], BF16)
    vt = P.sbuf("p2_vt", [128, 4096], BF16)
    qh = P.sbuf("p2_qh", [128, 2048], BF16)
    Eb = P.sbuf("p2_E", [128, 4096], F32)
    EB = P.sbuf("p2_EB", [128, 4096], F32)
    bin2 = P.sbuf("p2_b2", [128, 4096], BF16)
    hmr = P.sbuf("p2_hm", [1, 4096], F32)
    hmb = P.sbuf("p2_hmb", [128, 4096], BF16)
    Pb = P.sbuf("p2_P", [128, 4096], BF16)
    pts = P.sbuf("p2_pt", [128, 2 * 1024], BF16)
    zp = P.sbuf("p2_zp", [128, 8], F32)
    zs = P.sbuf("p2_zs", [128, 2], F32)
    onesrow = P.t["onesrow"]
    psKs = [P.t["at_psC0"], P.t["at_psC1"]]
    psA = P.t["p1_psA"]; psB = P.t["p1_psB"]
    psO = [P.t["at_psP0"], P.t["at_psP1"]]
    SCALE = float(1.0 / np.sqrt(128.0))
    sL, sK, sT, sQm, sE, sZ, sB2, sP, sPT, sAV, sAO = (f"p2{x}" for x in
        ("L", "K", "T", "Q", "E", "Z", "B", "P", "t", "V", "O"))
    for s in (sL, sK, sT, sQm, sE, sZ, sB2, sP, sPT, sAV, sAO):
        P.sem(s)
    gch = 0
    gpt = 0
    for hh in range(16):
        qm_head = P.cnt[sQm]
        av_head = P.cnt[sAV]
        P.emit("sync", lambda h, hh=hh: h.dma_start(out=kt[:], in_=kxg_d[hh, :, :]),
               waits=list(deps) + ([(sT, 32 * hh)] if hh > 0 else []), inc=(sL, 16))
        P.emit("sync", lambda h, hh=hh: h.dma_start(out=vt[:], in_=vxg_d[hh, :, :]),
               waits=([(sAV, av_head)] if hh > 0 else []), inc=(sL, 16))
        P.emit("sync", lambda h, hh=hh: h.dma_start(
            out=qh[:], in_=q_d[hh * 128:(hh + 1) * 128, :]),
            waits=([(sQm, qm_head)] if hh > 0 else []), inc=(sL, 16))
        P.emit("scalar", lambda h, hh=hh: h.dma_start(
            out=hmr[:], in_=hm_d[hh:hh + 1, :]),
            waits=([(sB2, P.cnt["p2B"])] if hh > 0 else []), inc=("p2Lh", 16))
        ldh = P.cnt["p2Lh"]
        lds = P.cnt[sL]
        for g in range(32):
            half = g % 2
            tw = [(sL, lds)] if g == 0 else []
            if g == 0:
                tw.append((sPT, P.cnt[sPT]))
            if g >= 2:
                tw.append((sK, 32 * hh + g - 1))
            P.emit("tensor", lambda h, g=g, half=half: h.transpose(
                psKs[half][:, 0:128], kt[:, g * 128:(g + 1) * 128],
                ident[:]), waits=tw, inc=(sT, 1))
            kw = [(sT, P.cnt[sT])]
            if g == 0 and hh > 0:
                kw.append((sQm, qm_head))
            P.emit("vector", lambda h, g=g, half=half: h.tensor_copy(
                out=kf[:, g * 128:(g + 1) * 128],
                in_=psKs[half][:, 0:128]),
                waits=kw, inc=(sK, 1))
        kdone = P.cnt[sK]
        for c8 in range(8):
            bm = P.emit("tensor", lambda h, c8=c8: h.matmul(
                P.t["p1_psA"][:, 0:512], onesrow[:],
                hmr[0:1, c8 * 512:(c8 + 1) * 512], start=True, stop=True),
                waits=[("p2Lh", ldh)] + ([(sB2, P.cnt["p2B"])] if (hh + c8) > 0 else []),
                inc=(sQm, 1))
            P.emit("scalar", lambda h, c8=c8: h.activation(
                out=hmb[:, c8 * 512:(c8 + 1) * 512], in_=P.t["p1_psA"][:, 0:512],
                func=AF.Copy), waits=[(sQm, P.cnt[sQm])], inc=(sB2, 1))
        hm_done = P.cnt[sB2]
        for t in range(NTO):
            W = (2 * t + 2) * 128
            nkt = 2 * t + 2
            z_tile = P.cnt[sZ]
            p_tile = P.cnt[sP]
            nch = (W + 1023) // 1024
            nz = 0
            qm0 = P.cnt[sQm]
            for c in range(nch):
                w0 = c * 1024
                w1 = min(W, w0 + 1024)
                ps = psA if gch % 2 == 0 else psB
                nsub = (w1 - w0 + 511) // 512
                for si in range(nsub):
                    s2 = w0 + si * 512
                    sw = min(512, w1 - s2)
                    mw = []
                    if si == 0:
                        mw = [(sK, kdone), (sB2, hm_done)]
                        if gch >= 2:
                            mw.append((sE, gch - 1))
                    P.emit("tensor", lambda h, t=t, s2=s2, sw=sw, ps=ps, w0=w0: h.matmul(
                        ps[:, s2 - w0:s2 - w0 + sw], qh[:, t * 128:(t + 1) * 128],
                        kf[:, s2:s2 + sw], start=True, stop=True),
                        waits=mw, inc=((sQm, 1) if si == nsub - 1 else None))
                qm = P.cnt[sQm]
                aw = [(sQm, qm)]
                if c == 0:
                    aw.append((sZ, z_tile))
                P.emit("scalar", lambda h, w0=w0, w1=w1, ps=ps: h.activation(
                    out=Eb[:, w0:w1], in_=ps[:, 0:w1 - w0], func=AF.Exp, scale=SCALE),
                    waits=aw, inc=(sE, 1))
                gch += 1
            ee = P.cnt[sE]
            b1 = P.emit("vector", lambda h, t=t, W=W: h.tensor_max(
                out=bin2[:, 0:W], in0=Tb_sb[:, 3840 - 2 * t * 128: 3840 - 2 * t * 128 + W],
                in1=hmb[:, 0:W]),
                waits=[(sE, ee), (sZ, z_tile)], inc=(sB2, 1))
            b2 = P.emit("vector", lambda h, t=t, W=W: h.tensor_mul(
                out=bin2[:, W - 256:W], in0=bin2[:, W - 256:W],
                in1=Mt_sb[:, t * 256:(t + 1) * 256]),
                waits=[(sB2, P.cnt[sB2])], inc=(sB2, 1))
            bb = P.cnt[sB2]
            e1 = P.emit("vector", lambda h, W=W: h.scalar_tensor_tensor(
                out=EB[:, 0:W], in0=Eb[:, 0:W], scalar=1.0, in1=bin2[:, 0:W],
                op0=ALU.mult, op1=ALU.mult, accum_out=zs[:, 1:2]),
                waits=[(sE, ee), (sB2, bb), (sP, p_tile)], inc=(sZ, 1))
            wv = P.emit("vector", lambda h: h.reciprocal(
                out=zs[:, 0:1], in_=zs[:, 1:2]), waits=[(sZ, P.cnt[sZ])], inc=(sZ, 1))
            pp = P.emit("scalar", lambda h, W=W: h.activation(
                out=Pb[:, 0:W], in_=EB[:, 0:W], func=AF.Copy, scale=zs[:, 0:1]),
                waits=[(sZ, P.cnt[sZ]), (sQm, P.cnt[sQm])], inc=(sP, 1))
            for bstart in range(0, nkt, 4):
                bcnt = min(4, nkt - bstart)
                pbuf = gpt % 2
                for j in range(bcnt):
                    tw2 = []
                    if j == 0:
                        tw2 = [(sP, P.cnt[sP]), (sPT, max(0, gpt - 1))]
                    P.emit("tensor", lambda h, pbuf=pbuf, j=j, bstart=bstart: h.transpose(
                        psKs[pbuf][:, j * 128:(j + 1) * 128],
                        Pb[:, (bstart + j) * 128:(bstart + j + 1) * 128], ident[:]),
                        waits=tw2, inc=((sQm, 1) if j == bcnt - 1 else None))
                ptq = P.cnt[sQm]
                P.emit("vector", lambda h, bcnt=bcnt, pbuf=pbuf: h.tensor_copy(
                    out=pts[:, pbuf * 1024:pbuf * 1024 + bcnt * 128],
                    in_=psKs[pbuf][:, 0:bcnt * 128]),
                    waits=[(sQm, ptq)] + ([(sAV, max(0, P.cnt[sAV] - 1))] if gpt >= 2 else []),
                    inc=(sPT, 1))
                ptd = P.cnt[sPT]
                po = psO[t % 2]
                for j in range(bcnt):
                    kti = bstart + j
                    mw2 = [(sPT, ptd)] if j == 0 else []
                    if bstart == 0 and j == 0:
                        mw2.append((sAO, P.cnt[sAO]))
                        mw2.append((sL, lds))
                    P.emit("tensor", lambda h, kti=kti, j=j, po=po, pbuf=pbuf, nkt=nkt: h.matmul(
                        po[:, 0:128], vt[:, kti * 128:(kti + 1) * 128],
                        pts[:, pbuf * 1024 + j * 128:pbuf * 1024 + (j + 1) * 128],
                        start=(kti == 0), stop=(kti == nkt - 1)),
                        waits=mw2, inc=((sAV, 1) if kti == nkt - 1 else None))
                gpt += 1
            avd = P.cnt[sAV]
            P.emit("scalar", lambda h, hh=hh, t=t, po=psO[t % 2]: h.activation(
                out=ao_all[:, hh * 2048 + t * 128: hh * 2048 + (t + 1) * 128],
                in_=po[:, 0:128], func=AF.Copy),
                waits=[(sAV, avd)], inc=(sAO, 1))
    return (sAO, P.cnt[sAO])


def emit_out_proj(P, tag, rhs_fn, strips, ncin, x_dram, xo_dram, scale_const, deps):
    assert ncin <= 16
    d_s = P.sbuf(f"{tag}_ds", [128, 2 * 2048], BF16)
    xt = P.sbuf(f"{tag}_xt", [128, 2 * 512], F32)
    xov = P.sbuf(f"{tag}_xo", [128, 2 * 512], F32)
    psY = [P.t["at_psP0"], P.t["at_psP1"]]
    sD, sYP, sXL, sYE, sST = (f"{tag}{x}" for x in "DPXES")
    for s in (sYP, sYE):
        P.sem(s)
    for tq in range(4):
        for oc in range(NF):
            buf = oc % 2
            it = tq * NF + oc
            sDb = sD + str(buf)
            P.emit("sync", lambda h, oc=oc, buf=buf: h.dma_start(
                out=d_s[:, buf * 2048:buf * 2048 + ncin * 128],
                in_=strips[oc, :, 0:ncin * 128]),
                waits=list(deps) + ([(sYP, it - 1)] if it >= 2 else []),
                inc=(sDb, 16))
            dready = P.cnt[sDb]
            sXLb = sXL + str(buf)
            P.emit("scalar", lambda h, oc=oc, tq=tq, buf=buf: h.dma_start(
                out=xt[:, buf * 512:(buf + 1) * 512],
                in_=x_dram[oc * 128:(oc + 1) * 128, tq * 512:(tq + 1) * 512]),
                waits=([(sYE, it - 1)] if it >= 2 else []), inc=(sXLb, 16))
            xready = P.cnt[sXLb]
            for ic in range(ncin):
                mw = []
                if ic == 0:
                    mw = [(sDb, dready)] + list(deps)
                    if it >= 2:
                        mw.append((sYE, it - 1))
                P.emit("tensor", lambda h, ic=ic, tq=tq, buf=buf: h.matmul(
                    psY[buf][:], d_s[:, buf * 2048 + ic * 128:buf * 2048 + (ic + 1) * 128],
                    rhs_fn(ic, tq),
                    start=(ic == 0), stop=(ic == ncin - 1)),
                    waits=mw, inc=((sYP, 1) if ic == ncin - 1 else None))
            ype = P.cnt[sYP]
            ce = P.emit("vector", lambda h, buf=buf: h.scalar_tensor_tensor(
                out=xov[:, buf * 512:(buf + 1) * 512], in0=psY[buf][:],
                scalar=float(scale_const), in1=xt[:, buf * 512:(buf + 1) * 512],
                op0=ALU.mult, op1=ALU.add),
                waits=[(sYP, ype), (sXLb, xready)] +
                      ([(sST + str(buf), (it // 2) * 16)] if it >= 2 else []),
                inc=(sYE, 1))
            P.emit("sync", lambda h, oc=oc, tq=tq, buf=buf: h.dma_start(
                out=xo_dram[oc * 128:(oc + 1) * 128, tq * 512:(tq + 1) * 512],
                in_=xov[:, buf * 512:(buf + 1) * 512]),
                waits=[(sYE, P.cnt[sYE])], inc=(sST + str(buf), 16))
    P.sem(sST + "0"); P.sem(sST + "1")
    return [(sST + "0", P.cnt[sST + "0"]), (sST + "1", P.cnt[sST + "1"])]


def build_l0_proj():
    P = Prog("l0proj")
    x = P.dram("x", [DM, TOWN], F32, kind="ExternalInput")
    wq = P.dram("wq", [NF, 128, 2048], F32, kind="ExternalInput")
    wk = P.dram("wk", [NF, 128, 2048], F32, kind="ExternalInput")
    wv = P.dram("wv", [NF, 128, 2048], F32, kind="ExternalInput")
    cosd = P.dram("cosd", [128, TOWN], F32, kind="ExternalInput")
    sind = P.dram("sind", [128, TOWN], F32, kind="ExternalInput")
    ones_d = P.dram("ones_d", [128, 128], F32, kind="ExternalInput")
    ident_d = P.dram("ident_d", [128, 128], F32, kind="ExternalInput")
    q_d = P.dram("q", [DM, TOWN], BF16, kind="ExternalOutput")
    kx_d = P.dram("kx", [NF, 128, TOWN], BF16, kind="ExternalOutput")
    vx_d = P.dram("vx", [NF, 128, TOWN], BF16, kind="ExternalOutput")
    wq_bf = P.dram("wq_bf", [NF, 128, 2048], BF16)
    wk_bf = P.dram("wk_bf", [NF, 128, 2048], BF16)
    wv_bf = P.dram("wv_bf", [NF, 128, 2048], BF16)

    P.sbuf("ones128", [128, 128], F32)
    P.sbuf("ident", [128, 128], BF16)
    cosT = P.sbuf("cosT", [128, TOWN], F32)
    sinT = P.sbuf("sinT", [128, TOWN], F32)
    y_sb = P.sbuf("y", [128, NF * 2048], BF16)
    alloc_attn_common(P)

    P.emit("sync", lambda h: h.dma_start(out=P.t["ones128"][:], in_=ones_d[:, :]), inc=("cL", 16))
    P.emit("gpsimd", lambda h: h.dma_start(out=P.t["ident"][:], in_=ident_d[:, :]), inc=("gL", 16))
    P.emit("sync", lambda h: h.dma_start(out=cosT[:], in_=cosd[:, :]), inc=("cL", 16))
    P.emit("sync", lambda h: h.dma_start(out=sinT[:], in_=sind[:, :]), inc=("cL", 16))
    cl = P.cnt["cL"]
    gl = P.cnt["gL"]
    cdep = emit_conv_cast(P, "cv", [(wq, wq_bf, 4), (wk, wk_bf, 4), (wv, wv_bf, 4)])
    ydep = emit_ln(P, "ln", x, y_sb, deps=[("cL", cl)])
    dq = emit_proj_kind(P, "aq", y_sb, wq_bf, q_d, True, False, cosT, sinT,
                        [ydep, cdep, ("cL", cl), ("gL", gl)])
    dk = emit_proj_kind(P, "ak", y_sb, wk_bf, kx_d, True, True, cosT, sinT, list(dq))
    dv = emit_proj_kind(P, "av", y_sb, wv_bf, vx_d, False, True, None, None, list(dk))
    for dep in (dq, dk, dv):
        for s_, v_ in dep:
            P.wait("sync", s_, v_)
    P.finalize()
    return P


def build_l0_p1():
    P = Prog("l0p1")
    q_d = P.dram("q", [DM, TOWN], BF16, kind="ExternalInput")
    kxg = P.dram("kxg", [NF, 128, TFULL], BF16, kind="ExternalInput")
    mt_d = P.dram("mt", [128, NTO * 256], F32, kind="ExternalInput")
    ones_d = P.dram("ones_d", [128, 128], F32, kind="ExternalInput")
    ident_d = P.dram("ident_d", [128, 128], F32, kind="ExternalInput")
    ts_d = P.dram("ts", [16, TFULL], F32, kind="ExternalOutput")
    P.sbuf("ones128", [128, 128], F32)
    P.sbuf("ident", [128, 128], BF16)
    Mt_sb = P.sbuf("mt_sb", [128, NTO * 256], BF16)
    P.psum("at_psP0", [128, 512], F32)
    P.psum("at_psC0", [128, 512], BF16)
    P.psum("at_psC1", [128, 512], BF16)
    P.psum("p1_psA", [128, 1024], F32)
    P.psum("p1_psB", [128, 1024], F32)
    P.emit("sync", lambda h: h.dma_start(out=P.t["ones128"][:], in_=ones_d[:, :]), inc=("cL", 16))
    P.emit("gpsimd", lambda h: h.dma_start(out=P.t["ident"][:], in_=ident_d[:, :]), inc=("gL", 16))
    P.emit("gpsimd", lambda h: h.dma_start(out=Mt_sb[:], in_=mt_d[:, :]), inc=("gL", 16))
    cl = P.cnt["cL"]
    gl = P.cnt["gL"]
    st = emit_l0_pass1(P, q_d, kxg, Mt_sb, ts_d, deps=[("cL", cl), ("gL", gl)])
    P.wait("sync", st[0], st[1])
    P.finalize()
    return P


def build_l0_p2():
    P = Prog("l0p2")
    q_d = P.dram("q", [DM, TOWN], BF16, kind="ExternalInput")
    kxg = P.dram("kxg", [NF, 128, TFULL], BF16, kind="ExternalInput")
    vxg = P.dram("vxg", [NF, 128, TFULL], BF16, kind="ExternalInput")
    ts_s = P.dram("ts_s", [16, TFULL], F32, kind="ExternalInput")
    ts_p = P.dram("ts_p", [16, TFULL], F32, kind="ExternalInput")
    mt_d = P.dram("mt", [128, NTO * 256], F32, kind="ExternalInput")
    tb_d = P.dram("tb", [128, 4096], F32, kind="ExternalInput")
    blk8_d = P.dram("blk8_d", [128, 16], F32, kind="ExternalInput")
    bc16_d = P.dram("bc16_d", [16, 128], F32, kind="ExternalInput")
    onesrow_d = P.dram("onesrow_d", [1, 128], F32, kind="ExternalInput")
    ident_d = P.dram("ident_d", [128, 128], F32, kind="ExternalInput")
    wo = P.dram("wo", [NF, 128, 2048], F32, kind="ExternalInput")
    x = P.dram("x", [DM, TOWN], F32, kind="ExternalInput")
    xo = P.dram("xo", [DM, TOWN], F32, kind="ExternalOutput")
    hm_d = P.dram("hm", [16, TFULL], F32)
    wo_bf = P.dram("wo_bf", [NF, 128, 2048], BF16)

    P.sbuf("ident", [128, 128], BF16)
    P.sbuf("blk8_sb", [128, 16], F32)
    P.sbuf("bc16_sb", [16, 128], F32)
    P.sbuf("onesrow", [1, 128], F32)
    Mt_sb = P.sbuf("mt_sb", [128, NTO * 256], BF16)
    Tb_sb = P.sbuf("tb_sb", [128, 4096], BF16)
    ao_all = P.sbuf("ao_all", [128, NF * 2048], BF16)
    P.psum("at_psP0", [128, 512], F32)
    P.psum("at_psP1", [128, 512], F32)
    P.psum("at_psC0", [128, 512], BF16)
    P.psum("at_psC1", [128, 512], BF16)
    P.psum("p1_psA", [128, 1024], F32)
    P.psum("p1_psB", [128, 1024], F32)

    P.emit("gpsimd", lambda h: h.dma_start(out=P.t["ident"][:], in_=ident_d[:, :]), inc=("cL", 16))
    P.emit("gpsimd", lambda h: h.dma_start(out=P.t["blk8_sb"][:], in_=blk8_d[:, :]), inc=("cL", 16))
    P.emit("gpsimd", lambda h: h.dma_start(out=P.t["bc16_sb"][:], in_=bc16_d[:, :]), inc=("cL", 16))
    P.emit("gpsimd", lambda h: h.dma_start(out=P.t["onesrow"][:], in_=onesrow_d[:, :]), inc=("cL", 16))
    P.emit("gpsimd", lambda h: h.dma_start(out=Mt_sb[:], in_=mt_d[:, :]), inc=("cL", 16))
    P.emit("gpsimd", lambda h: h.dma_start(out=Tb_sb[:], in_=tb_d[:, :]), inc=("cL", 16))
    cl = P.cnt["cL"]
    cdep = emit_conv_cast(P, "cv", [(wo, wo_bf, 4)], deps=[("cL", cl)])
    bdep = emit_bisect(P, ts_s, ts_p, hm_d, deps=[("cL", cl)])
    aodep = emit_l0_pass2(P, q_d, kxg, vxg, hm_d, Tb_sb, Mt_sb, ao_all,
                          deps=[("cL", cl), bdep])
    st = emit_out_proj(P, "wo",
                       lambda ic, tq: ao_all[:, ic * 2048 + tq * 512: ic * 2048 + (tq + 1) * 512],
                       wo_bf, NF, x, xo, 1.0, deps=[aodep, cdep])
    for s_, v_ in st:
        P.wait("sync", s_, v_)
    P.finalize()
    return P


def prep_l0_attn(inp):
    cos_l0, sin_l0 = rope_tables_l0()
    out = {}
    out["wq"] = tile_lhsT(np.ascontiguousarray(inp["L0_attn_Wq"].T)).reshape(NF, 128, 2048)
    out["wk"] = tile_lhsT(np.ascontiguousarray(inp["L0_attn_Wk"].T)).reshape(NF, 128, 2048)
    out["wv"] = tile_lhsT(np.ascontiguousarray(inp["L0_attn_Wv"].T)).reshape(NF, 128, 2048)
    out["wo"] = tile_lhsT(np.ascontiguousarray(inp["L0_attn_Wo"].T)).reshape(NF, 128, 2048)
    out["ones_d"] = np.ones((128, 128), np.float32)
    out["ident_d"] = np.eye(128, dtype=np.float32)
    out["onesrow_d"] = np.ones((1, 128), np.float32)
    p_ = np.arange(128)
    out["blk8_d"] = (p_[:, None] // 8 == np.arange(16)[None, :]).astype(np.float32)
    out["bc16_d"] = (np.arange(16)[:, None] == p_[None, :] // 8).astype(np.float32)
    out["cos_p"] = [np.ascontiguousarray(cos_l0[_OWN[p]].T) for p in range(2)]
    out["sin_p"] = [np.ascontiguousarray(sin_l0[_OWN[p]].T) for p in range(2)]
    tril = np.tril(np.ones((128, 128), np.float32))
    mts = []
    for p in range(2):
        M = np.zeros((128, NTO * 256), np.float32)
        for t in range(NTO):
            if p == 0:
                M[:, t * 256:t * 256 + 128] = tril
            else:
                M[:, t * 256:t * 256 + 128] = 1.0
                M[:, t * 256 + 128:t * 256 + 256] = tril
        mts.append(M)
    out["mt_p"] = mts
    v_ = np.arange(4096)
    tbs = []
    for p in range(2):
        d = p * 128 + p_[:, None] + 3840 - v_[None, :]
        tbs.append((np.abs(d) <= 409).astype(np.float32))
    out["tb_p"] = tbs
    return out

def interleave_kx(k0, k1):
    g = np.zeros((NF, 128, TFULL), k0.dtype)
    for t in range(NTO):
        g[:, :, (2 * t) * 128:(2 * t + 1) * 128] = k0[:, :, t * 128:(t + 1) * 128]
        g[:, :, (2 * t + 1) * 128:(2 * t + 2) * 128] = k1[:, :, t * 128:(t + 1) * 128]
    return g


def emit_ck(P, y_sb, wc_bf, ckx_d, knw_ap, deps):
    ident = P.t["ident"]
    ones = P.t["ones128"]
    ck_ps = P.psum("ck_ps", [128, 128], F32)
    psP = [ck_ps, ck_ps]
    psS = P.t["at_psS"]
    psC0 = P.t["at_psC0"]
    wcs = P.sbuf("ck_ws", [128, 2 * 2048], BF16)
    ckf = P.sbuf("ck_f", [128, 128], F32)
    cks = P.sbuf("ck_s", [128, 128], F32)
    ckb = P.sbuf("ck_b", [128, 128], BF16)
    cko = P.sbuf("ck_o", [128, 128], BF16)
    rr = P.t["at_rr"]
    sW, sM, sQ, sT, sS = ("ckW", "ckM", "ckQ", "ckT", "ckS")
    for s in (sW, sM, sQ, sT, sS):
        P.sem(s)
    for ss in range(16):
        buf = ss % 2
        P.emit("sync", lambda h, ss=ss, buf=buf: h.dma_start(
            out=wcs[:, buf * 2048:(buf + 1) * 2048],
            in_=wc_bf[0, :, ss * 2048:(ss + 1) * 2048]),
            waits=(list(deps) if ss < 2 else [("ckP", ss - 1)]),
            inc=("ckW" + str(buf), 16))
        wr = P.cnt["ckW" + str(buf)]
        for j in range(16):
            cc = ss * 16 + j
            m = cc // 16
            fpart = cc % 16
            mw = []
            if j == 0:
                mw = [("ckW" + str(buf), wr)]
                if cc == 0:
                    mw += list(deps)
            P.emit("tensor", lambda h, m=m, fpart=fpart, j=j, buf=buf, cc=cc: h.matmul(
                psP[0][:],
                wcs[:, buf * 2048 + j * 128:buf * 2048 + (j + 1) * 128],
                y_sb[:, fpart * 2048 + m: fpart * 2048 + m + 16 * 127 + 1:16],
                start=(cc == 0), stop=(cc == 255)),
                waits=mw,
                inc=(("ckP", 1) if j == 15 else None))
    P.emit("tensor", None)
    mm = P.cnt["ckP"]
    sM = "ckP"
    c1 = P.emit("scalar", lambda h: h.activation(
        out=ckf[:], in_=psP[0][:], func=AF.Copy), waits=[(sM, mm)], inc=(sQ, 1))
    c2 = P.emit("scalar", lambda h: h.activation(
        out=cks[:], in_=ckf[:], func=AF.Square), waits=[(sQ, c1)], inc=(sQ, 1))
    c3 = P.emit("tensor", lambda h: h.matmul(
        psS[:, 0:128], ones[:], cks[:], start=True, stop=True),
        waits=[(sQ, c2)], inc=(sM, 1))
    c4 = P.emit("vector", lambda h: h.scalar_tensor_tensor(
        out=cks[:], in0=psS[:, 0:128], scalar=1.0 / 128.0, in1=P.t["epsb"][:, 0:128],
        op0=ALU.mult, op1=ALU.add), waits=[(sM, P.cnt[sM])], inc=(sQ, 1))
    c5 = P.emit("scalar", lambda h: h.activation(
        out=cks[:], in_=cks[:], func=AF.Ln), waits=[(sQ, P.cnt[sQ])], inc=(sQ, 1))
    c6 = P.emit("scalar", lambda h: h.activation(
        out=cks[:], in_=cks[:], func=AF.Exp, scale=-0.5), waits=[(sQ, P.cnt[sQ])], inc=(sQ, 1))
    c7 = P.emit("vector", lambda h, knw_ap=knw_ap: h.scalar_tensor_tensor(
        out=ckb[:], in0=ckf[:], scalar=knw_ap, in1=cks[:],
        op0=ALU.mult, op1=ALU.mult), waits=[(sQ, P.cnt[sQ])], inc=(sQ, 1))
    c8 = P.emit("tensor", lambda h: h.transpose(psC0[:, 0:128], ckb[:], ident[:]),
                waits=[(sQ, P.cnt[sQ])], inc=(sM, 1))
    c9 = P.emit("scalar", lambda h: h.activation(
        out=cko[:], in_=psC0[:, 0:128], func=AF.Copy), waits=[(sM, P.cnt[sM])], inc=(sQ, 1))
    P.emit("sync", lambda h: h.dma_start(out=ckx_d[0, :, :], in_=cko[:]),
           waits=[(sQ, P.cnt[sQ])], inc=(sS, 16))
    return (sS, P.cnt[sS])


def emit_l1_att(P, q_d, kx1, vx1, ckx2, tw_sb, cv_sb, sinkb_sb, ao_all, deps):
    ident = P.t["ident"]
    kf = P.sbuf("a1_kf", [128, 512 + 4096], BF16)
    kt = P.sbuf("a1_kt", [128, 4096], BF16)
    vt = P.sbuf("a1_vt", [128, 4096], BF16)
    ckt = P.sbuf("a1_ckt", [128, 256], BF16)
    ckf = P.sbuf("a1_ckf", [128, 256], BF16)
    qh = P.sbuf("a1_qh", [128, 2048], BF16)
    tts = P.sbuf("a1_tt", [128, 768], F32)
    Es = P.sbuf("a1_Es", [128, 768], F32)
    EBs = P.sbuf("a1_EBs", [128, 768], F32)
    Ec = P.sbuf("a1_Ec", [128, 256], F32)
    EBc = P.sbuf("a1_EBc", [128, 256], F32)
    Pbs = P.sbuf("a1_Ps", [128, 768], BF16)
    Pbc = P.sbuf("a1_Pc", [128, 256], BF16)
    pts = P.sbuf("a1_pts", [128, 1024], BF16)
    zz = P.sbuf("a1_zz", [128, 8], F32)
    sinkexp = P.sbuf("a1_se", [128, 16], F32)
    psW = P.t["p1_psA"]
    psC2 = P.t["p1_psB"]
    psKs = [P.t["at_psC0"], P.t["at_psC1"]]
    psO = [P.t["at_psP0"], P.t["at_psP1"]]
    SCALE = float(1.0 / np.sqrt(128.0))
    CAP = 50.0
    sL, sK, sT, sQm, sE, sZ, sP, sPT, sAV, sAO, sB2 = (f"a1{x}" for x in
        ("L", "K", "T", "Q", "E", "Z", "P", "t", "V", "O", "B"))
    for s in (sL, sK, sT, sQm, sE, sZ, sP, sPT, sAV, sAO, sB2):
        P.sem(s)
    P.emit("sync", lambda h: h.dma_start(out=kt[:], in_=kx1[:, :]),
           waits=list(deps), inc=(sL, 16))
    P.emit("sync", lambda h: h.dma_start(out=vt[:], in_=vx1[:, :]),
           waits=list(deps), inc=(sL, 16))
    P.emit("sync", lambda h: h.dma_start(out=ckt[:, 0:128], in_=ckx2[0, :, :]),
           waits=list(deps), inc=(sL, 16))
    P.emit("sync", lambda h: h.dma_start(out=ckt[:, 128:256], in_=ckx2[1, :, :]),
           waits=list(deps), inc=(sL, 16))
    lds = P.cnt[sL]
    zm = P.emit("vector", lambda h: h.memset(kf[:, 0:512], 0.0), inc=(sK, 1))
    P.emit("scalar", lambda h: h.activation(
        out=sinkexp[:], in_=sinkb_sb[:], func=AF.Exp),
        waits=list(deps), inc=(sB2, 1))
    for g in range(32):
        half = g % 2
        tw = [(sL, lds)] if g == 0 else []
        if g >= 2:
            tw.append((sK, 1 + g - 1))
        P.emit("tensor", lambda h, g=g, half=half: h.transpose(
            psKs[half][:, 0:128], kt[:, g * 128:(g + 1) * 128], ident[:]),
            waits=tw, inc=(sT, 1))
        P.emit("vector", lambda h, g=g, half=half: h.tensor_copy(
            out=kf[:, 512 + g * 128:512 + (g + 1) * 128], in_=psKs[half][:, 0:128]),
            waits=[(sT, P.cnt[sT])], inc=(sK, 1))
    for g2 in range(2):
        P.emit("tensor", lambda h, g2=g2: h.transpose(
            psKs[g2][:, 0:128], ckt[:, g2 * 128:(g2 + 1) * 128], ident[:]),
            waits=[(sL, lds), (sK, 33)], inc=(sT, 1))
        P.emit("vector", lambda h, g2=g2: h.tensor_copy(
            out=ckf[:, g2 * 128:(g2 + 1) * 128], in_=psKs[g2][:, 0:128]),
            waits=[(sT, P.cnt[sT])], inc=(sK, 1))
    kdone = P.cnt[sK]
    gpt = 0
    for hh in range(16):
        qm_head = P.cnt[sQm]
        av_head = P.cnt[sAV]
        P.emit("sync", lambda h, hh=hh: h.dma_start(
            out=qh[:], in_=q_d[hh * 128:(hh + 1) * 128, :]),
            waits=([(sQm, qm_head)] if hh > 0 else [(sL, lds)]), inc=(sL, 16))
        lq = P.cnt[sL]
        for t in range(NTO):
            z_tile = P.cnt[sZ]
            p_tile = P.cnt[sP]
            koff = 512 + (2 * t - 4) * 128
            mw = [(sK, kdone), (sL, lq)]
            if hh + t > 0:
                mw.append((sE, P.cnt[sE]))
            P.emit("tensor", lambda h, t=t, koff=koff: h.matmul(
                psW[:, 0:512], qh[:, t * 128:(t + 1) * 128], kf[:, koff:koff + 512],
                start=True, stop=True), waits=mw, inc=None)
            P.emit("tensor", lambda h, t=t, koff=koff: h.matmul(
                psW[:, 512:768], qh[:, t * 128:(t + 1) * 128],
                kf[:, koff + 512:koff + 768],
                start=True, stop=True), inc=(sQm, 1))
            P.emit("tensor", lambda h, t=t: h.matmul(
                psC2[:, 0:256], qh[:, t * 128:(t + 1) * 128], ckf[:, 0:256],
                start=True, stop=True), inc=(sQm, 1))
            qm = P.cnt[sQm]
            a1 = P.emit("scalar", lambda h: h.activation(
                out=tts[:], in_=psW[:, 0:768], func=AF.Tanh, scale=SCALE / CAP),
                waits=[(sQm, qm - 1), (sZ, z_tile)], inc=(sE, 1))
            a2 = P.emit("scalar", lambda h: h.activation(
                out=Es[:], in_=tts[:], func=AF.Exp, scale=CAP),
                waits=[(sE, P.cnt[sE])], inc=(sE, 1))
            twslice = 2 if t >= 2 else t
            e1 = P.emit("vector", lambda h, tw2=twslice: h.scalar_tensor_tensor(
                out=EBs[:], in0=Es[:], scalar=1.0, in1=tw_sb[:, tw2 * 768:(tw2 + 1) * 768],
                op0=ALU.mult, op1=ALU.mult, accum_out=zz[:, 0:1]),
                waits=[(sE, P.cnt[sE]), (sP, p_tile)], inc=(sZ, 1))
            m1 = P.emit("vector", lambda h: h.reduce_max(
                out=zz[:, 2:3], in_=psC2[:, 0:256], axis=AX.X),
                waits=[(sQm, qm), (sZ, z_tile)], inc=(sZ, 1))
            m2 = P.emit("vector", lambda h: h.tensor_scalar_mul(
                out=zz[:, 2:3], in0=zz[:, 2:3], scalar1=SCALE),
                waits=[(sZ, P.cnt[sZ])], inc=(sZ, 1))
            m3 = P.emit("vector", lambda h, hh=hh: h.tensor_max(
                out=zz[:, 2:3], in0=zz[:, 2:3], in1=sinkb_sb[:, hh:hh + 1]),
                waits=[(sZ, P.cnt[sZ]), (sB2, 1)], inc=(sZ, 1))
            m4 = P.emit("vector", lambda h: h.tensor_scalar_mul(
                out=zz[:, 3:4], in0=zz[:, 2:3], scalar1=-1.0),
                waits=[(sZ, P.cnt[sZ])], inc=(sZ, 1))
            a3 = P.emit("scalar", lambda h: h.activation(
                out=Ec[:], in_=psC2[:, 0:256], func=AF.Exp, scale=SCALE,
                bias=zz[:, 3:4]),
                waits=[(sZ, P.cnt[sZ])], inc=(sE, 1))
            e2 = P.emit("vector", lambda h, t=t: h.scalar_tensor_tensor(
                out=EBc[:], in0=Ec[:], scalar=1.0, in1=cv_sb[:, t * 256:(t + 1) * 256],
                op0=ALU.mult, op1=ALU.mult, accum_out=zz[:, 1:2]),
                waits=[(sE, P.cnt[sE]), (sP, p_tile)], inc=(sZ, 1))
            a4 = P.emit("scalar", lambda h, hh=hh: h.activation(
                out=zz[:, 4:5], in_=zz[:, 3:4], func=AF.Exp,
                bias=sinkb_sb[:, hh:hh + 1]),
                waits=[(sZ, P.cnt[sZ])], inc=(sE, 1))
            z1 = P.emit("vector", lambda h, hh=hh: h.tensor_add(
                out=zz[:, 0:1], in0=zz[:, 0:1], in1=sinkexp[:, hh:hh + 1]),
                waits=[(sZ, P.cnt[sZ]), (sB2, 1)], inc=(sZ, 1))
            z2 = P.emit("vector", lambda h: h.tensor_add(
                out=zz[:, 1:2], in0=zz[:, 1:2], in1=zz[:, 4:5]),
                waits=[(sE, P.cnt[sE]), (sZ, P.cnt[sZ])], inc=(sZ, 1))
            w1 = P.emit("vector", lambda h: h.reciprocal(
                out=zz[:, 5:6], in_=zz[:, 0:1]), waits=[(sZ, P.cnt[sZ])], inc=(sZ, 1))
            w2 = P.emit("vector", lambda h: h.reciprocal(
                out=zz[:, 6:7], in_=zz[:, 1:2]), waits=[(sZ, P.cnt[sZ])], inc=(sZ, 1))
            p1 = P.emit("scalar", lambda h: h.activation(
                out=Pbs[:], in_=EBs[:], func=AF.Copy, scale=zz[:, 5:6]),
                waits=[(sZ, P.cnt[sZ]), (sQm, P.cnt[sQm])], inc=(sP, 1))
            p2 = P.emit("scalar", lambda h: h.activation(
                out=Pbc[:], in_=EBc[:], func=AF.Copy, scale=zz[:, 6:7]),
                waits=[(sZ, P.cnt[sZ]), (sQm, P.cnt[sQm])], inc=(sP, 1))
            pdone = P.cnt[sP]
            for bstart in range(0, 8, 4):
                pbuf = gpt % 2
                for j in range(4):
                    ci = bstart + j
                    tw2 = []
                    if j == 0:
                        tw2 = [(sP, pdone), (sPT, max(0, gpt - 1))]
                    if ci < 6:
                        src_ap = lambda h, ci=ci: Pbs[:, ci * 128:(ci + 1) * 128]
                    else:
                        src_ap = lambda h, ci=ci: Pbc[:, (ci - 6) * 128:(ci - 5) * 128]
                    P.emit("tensor", lambda h, j=j, pbuf=pbuf, src_ap=src_ap: h.transpose(
                        psKs[pbuf][:, j * 128:(j + 1) * 128], src_ap(h), ident[:]),
                        waits=tw2, inc=((sQm, 1) if j == 3 else None))
                P.emit("vector", lambda h, pbuf=pbuf, bstart=bstart: h.tensor_copy(
                    out=pts[:, bstart * 128:(bstart + 4) * 128],
                    in_=psKs[pbuf][:, 0:512]),
                    waits=[(sQm, P.cnt[sQm])] +
                          ([(sAV, P.cnt[sAV])] if gpt >= 2 else []),
                    inc=(sPT, 1))
                gpt += 1
            ptd = P.cnt[sPT]
            po = psO[t % 2]
            valid = [i for i in range(6) if (2 * t - 4 + i) >= 0]
            nav = len(valid) + 2
            navi = 0
            for i in valid:
                gkt = 2 * t - 4 + i
                mw2 = []
                if navi == 0:
                    mw2 = [(sPT, ptd), (sAO, P.cnt[sAO]), (sL, lq)]
                P.emit("tensor", lambda h, gkt=gkt, i=i, po=po, navi=navi: h.matmul(
                    po[:, 0:128], vt[:, gkt * 128:(gkt + 1) * 128],
                    pts[:, i * 128:(i + 1) * 128],
                    start=(navi == 0), stop=False),
                    waits=mw2, inc=None)
                navi += 1
            for g2 in range(2):
                P.emit("tensor", lambda h, g2=g2, po=po: h.matmul(
                    po[:, 0:128], ckt[:, g2 * 128:(g2 + 1) * 128],
                    pts[:, (6 + g2) * 128:(7 + g2) * 128],
                    start=False, stop=(g2 == 1)),
                    inc=((sAV, 1) if g2 == 1 else None))
            avd = P.cnt[sAV]
            P.emit("scalar", lambda h, hh=hh, t=t, po=po: h.activation(
                out=ao_all[:, hh * 2048 + t * 128: hh * 2048 + (t + 1) * 128],
                in_=po[:, 0:128], func=AF.Copy),
                waits=[(sAV, avd)], inc=(sAO, 1))
    return (sAO, P.cnt[sAO])


def emit_l1_out(P, ao_all, wg_bf, wo_bf, x_dram, xo_dram, deps):
    wgs = P.sbuf("l1o_wgs", [128, 2 * 512], BF16)
    wos = P.sbuf("l1o_wos", [128, 2 * 8192], BF16)
    inter = P.sbuf("l1o_int", [128, 64 * 256], BF16)
    xt = P.sbuf("l1o_xt", [128, 2 * 256], F32)
    xov = P.sbuf("l1o_xo", [128, 2 * 256], F32)
    psY = [P.t["at_psP0"], P.t["at_psP1"]]
    psI = [P.t["p1_psA"], P.t["p1_psB"]]
    sGW, sIP, sIE, sOW, sYP, sXL, sYE, sST = (f"l1o{x}" for x in
        ("G", "g", "e", "W", "P", "X", "E", "S"))
    for s in (sGW, sIP, sIE, sOW, sYP, sXL, sYE, sST):
        P.sem(s)
    for th in range(8):
        t0c = th * 256
        for gc in range(64):
            g = gc // 16
            o = gc % 16
            buf = gc % 2
            it = th * 64 + gc
            sGWb = sGW + str(buf)
            P.emit("sync", lambda h, gc=gc, buf=buf: h.dma_start(
                out=wgs[:, buf * 512:(buf + 1) * 512], in_=wg_bf[gc, :, :]),
                waits=(list(deps) if it < 2 else [(sIP, it - 1)]), inc=(sGWb, 16))
            wr = P.cnt[sGWb]
            for j in range(4):
                mw = []
                if j == 0:
                    mw = [(sGWb, wr)] + (list(deps) if it == 0 else [])
                    if it >= 2:
                        mw.append((sIE, it - 1))
                P.emit("tensor", lambda h, g=g, j=j, t0c=t0c, buf=buf: h.matmul(
                    psI[buf][:, 0:256],
                    wgs[:, buf * 512 + j * 128:buf * 512 + (j + 1) * 128],
                    ao_all[:, (4 * g + j) * 2048 + t0c:(4 * g + j) * 2048 + t0c + 256],
                    start=(j == 0), stop=(j == 3)),
                    waits=mw, inc=((sIP, 1) if j == 3 else None))
            ip = P.cnt[sIP]
            P.emit("scalar", lambda h, gc=gc, buf=buf: h.activation(
                out=inter[:, gc * 256:(gc + 1) * 256], in_=psI[buf][:, 0:256],
                func=AF.Copy), waits=[(sIP, ip)], inc=(sIE, 1))
        idone = P.cnt[sIE]
        for oc in range(NF):
            buf = oc % 2
            it = th * NF + oc
            sOWb = sOW + str(buf)
            P.emit("sync", lambda h, oc=oc, buf=buf: h.dma_start(
                out=wos[:, buf * 8192:(buf + 1) * 8192], in_=wo_bf[oc, :, :]),
                waits=([(sYP, it - 1)] if it >= 2 else list(deps)), inc=(sOWb, 16))
            owr = P.cnt[sOWb]
            sXLb = sXL + str(buf)
            P.emit("scalar", lambda h, oc=oc, t0c=t0c, buf=buf: h.dma_start(
                out=xt[:, buf * 256:(buf + 1) * 256],
                in_=x_dram[oc * 128:(oc + 1) * 128, t0c:t0c + 256]),
                waits=([(sYE, it - 1)] if it >= 2 else []), inc=(sXLb, 16))
            xr = P.cnt[sXLb]
            for ic in range(64):
                mw = []
                if ic == 0:
                    mw = [(sOWb, owr), (sIE, idone)]
                    if it >= 2:
                        mw.append((sYE, it - 1))
                P.emit("tensor", lambda h, ic=ic, oc=oc, buf=buf: h.matmul(
                    psY[buf][:, 0:256],
                    wos[:, buf * 8192 + ic * 128:buf * 8192 + (ic + 1) * 128],
                    inter[:, ic * 256:(ic + 1) * 256],
                    start=(ic == 0), stop=(ic == 63)),
                    waits=mw, inc=((sYP, 1) if ic == 63 else None))
            ype = P.cnt[sYP]
            ce = P.emit("vector", lambda h, buf=buf: h.scalar_tensor_tensor(
                out=xov[:, buf * 256:(buf + 1) * 256], in0=psY[buf][:, 0:256],
                scalar=1.0, in1=xt[:, buf * 256:(buf + 1) * 256],
                op0=ALU.mult, op1=ALU.add),
                waits=[(sYP, ype), (sXLb, xr)] +
                      ([(sST + str(buf), (it // 2) * 16)] if it >= 2 else []),
                inc=(sYE, 1))
            P.emit("sync", lambda h, oc=oc, t0c=t0c, buf=buf: h.dma_start(
                out=xo_dram[oc * 128:(oc + 1) * 128, t0c:t0c + 256],
                in_=xov[:, buf * 256:(buf + 1) * 256]),
                waits=[(sYE, P.cnt[sYE])], inc=(sST + str(buf), 16))
    P.sem(sST + "0"); P.sem(sST + "1")
    return [(sST + "0", P.cnt[sST + "0"]), (sST + "1", P.cnt[sST + "1"])]


def build_l1_proj():
    P = Prog("l1proj")
    x = P.dram("x", [DM, TOWN], F32, kind="ExternalInput")
    wqc = P.dram("wqc", [4, 128, 2048], F32, kind="ExternalInput")
    wqu = P.dram("wqu", [NF, 128, 512], F32, kind="ExternalInput")
    wk1 = P.dram("wk1", [1, 128, 2048], F32, kind="ExternalInput")
    wv1 = P.dram("wv1", [1, 128, 2048], F32, kind="ExternalInput")
    wcm = P.dram("wcm", [1, 128, 256 * 128], F32, kind="ExternalInput")
    qnw = P.dram("qnw", [128, 1], F32, kind="ExternalInput")
    knw = P.dram("knw", [128, 1], F32, kind="ExternalInput")
    cosd = P.dram("cosd", [128, TOWN], F32, kind="ExternalInput")
    sind = P.dram("sind", [128, TOWN], F32, kind="ExternalInput")
    ones_d = P.dram("ones_d", [128, 128], F32, kind="ExternalInput")
    ident_d = P.dram("ident_d", [128, 128], F32, kind="ExternalInput")
    q_d = P.dram("q", [DM, TOWN], BF16, kind="ExternalOutput")
    kx_d = P.dram("kx", [1, 128, TOWN], BF16, kind="ExternalOutput")
    vx_d = P.dram("vx", [1, 128, TOWN], BF16, kind="ExternalOutput")
    ckx_d = P.dram("ckx", [1, 128, 128], BF16, kind="ExternalOutput")
    wqc_bf = P.dram("wqc_bf", [4, 128, 2048], BF16)
    wqu_bf = P.dram("wqu_bf", [NF, 128, 512], BF16)
    wk1_bf = P.dram("wk1_bf", [1, 128, 2048], BF16)
    wv1_bf = P.dram("wv1_bf", [1, 128, 2048], BF16)
    wcm_bf = P.dram("wcm_bf", [1, 128, 256 * 128], BF16)

    P.sbuf("ones128", [128, 128], F32)
    P.sbuf("ident", [128, 128], BF16)
    cosT = P.sbuf("cosT", [128, TOWN], F32)
    sinT = P.sbuf("sinT", [128, TOWN], F32)
    qnw_sb = P.sbuf("qnw_sb", [128, 1], F32)
    knw_sb = P.sbuf("knw_sb", [128, 1], F32)
    y_sb = P.sbuf("y", [128, NF * 2048], BF16)
    qc_sb = P.sbuf("qc", [128, 4 * 2048], BF16)
    alloc_attn_common(P)

    P.emit("sync", lambda h: h.dma_start(out=P.t["ones128"][:], in_=ones_d[:, :]), inc=("cL", 16))
    P.emit("gpsimd", lambda h: h.dma_start(out=P.t["ident"][:], in_=ident_d[:, :]), inc=("gL", 16))
    P.emit("sync", lambda h: h.dma_start(out=cosT[:], in_=cosd[:, :]), inc=("cL", 16))
    P.emit("sync", lambda h: h.dma_start(out=sinT[:], in_=sind[:, :]), inc=("cL", 16))
    P.emit("sync", lambda h: h.dma_start(out=qnw_sb[:], in_=qnw[:, :]), inc=("cL", 16))
    P.emit("sync", lambda h: h.dma_start(out=knw_sb[:], in_=knw[:, :]), inc=("cL", 16))
    cl = P.cnt["cL"]; gl = P.cnt["gL"]
    ef = emit_fill_epsb(P, 1e-6)
    cdep = emit_conv_cast(P, "cv", [(wqc, wqc_bf, 2), (wqu, wqu_bf, 2),
                                    (wk1, wk1_bf, 1), (wv1, wv1_bf, 1),
                                    (wcm, wcm_bf, 1)])
    ydep = emit_ln(P, "ln", x, y_sb, deps=[("cL", cl)])
    base = [ydep, cdep, ("cL", cl), ("gL", gl), ("epsF", ef)]
    dqc = emit_proj_kind(P, "qc", y_sb, wqc_bf, None, False, False, None, None,
                         base, nout=4, to_sbuf=qc_sb)
    dq = emit_proj_kind(P, "q1", qc_sb, wqu_bf, q_d, True, False, cosT, sinT,
                        list(dqc), nout=NF, src_nf=4,
                        rms=(qnw_sb[:, 0:1], 1e-6, True))
    dk = emit_proj_kind(P, "k1", y_sb, wk1_bf, kx_d, True, True, cosT, sinT,
                        list(dq), nout=1, rms=(knw_sb[:, 0:1], 1e-6, True))
    dv = emit_proj_kind(P, "v1", y_sb, wv1_bf, vx_d, False, True, None, None,
                        list(dk), nout=1)
    dck = emit_ck(P, y_sb, wcm_bf, ckx_d, knw_sb[:, 0:1], list(dv) + [ydep])
    for dep in (dqc, dq, dk, dv):
        for s_, v_ in dep:
            P.wait("sync", s_, v_)
    P.wait("sync", dck[0], dck[1])
    P.finalize()
    return P


def build_l1_att():
    P = Prog("l1att")
    q_d = P.dram("q", [DM, TOWN], BF16, kind="ExternalInput")
    kx1 = P.dram("kx1", [128, TFULL], BF16, kind="ExternalInput")
    vx1 = P.dram("vx1", [128, TFULL], BF16, kind="ExternalInput")
    ckx2 = P.dram("ckx2", [2, 128, 128], BF16, kind="ExternalInput")
    tw_d = P.dram("tw", [128, 3 * 768], F32, kind="ExternalInput")
    cv_d = P.dram("cv", [128, NTO * 256], F32, kind="ExternalInput")
    sinkb_d = P.dram("sinkb", [128, 16], F32, kind="ExternalInput")
    ident_d = P.dram("ident_d", [128, 128], F32, kind="ExternalInput")
    wg = P.dram("wg", [64, 128, 512], F32, kind="ExternalInput")
    wo1 = P.dram("wo1", [NF, 128, 8192], F32, kind="ExternalInput")
    x = P.dram("x", [DM, TOWN], F32, kind="ExternalInput")
    xo = P.dram("xo", [DM, TOWN], F32, kind="ExternalOutput")
    wg_bf = P.dram("wg_bf", [64, 128, 512], BF16)
    wo1_bf = P.dram("wo1_bf", [NF, 128, 8192], BF16)

    P.sbuf("ident", [128, 128], BF16)
    tw_sb = P.sbuf("tw_sb", [128, 3 * 768], F32)
    cv_sb = P.sbuf("cv_sb", [128, NTO * 256], F32)
    sinkb_sb = P.sbuf("sinkb_sb", [128, 16], F32)
    ao_all = P.sbuf("ao_all", [128, NF * 2048], BF16)
    P.psum("at_psP0", [128, 512], F32)
    P.psum("at_psP1", [128, 512], F32)
    P.psum("at_psC0", [128, 512], BF16)
    P.psum("at_psC1", [128, 512], BF16)
    P.psum("p1_psA", [128, 1024], F32)
    P.psum("p1_psB", [128, 1024], F32)

    P.emit("gpsimd", lambda h: h.dma_start(out=P.t["ident"][:], in_=ident_d[:, :]), inc=("gL", 16))
    P.emit("sync", lambda h: h.dma_start(out=tw_sb[:], in_=tw_d[:, :]), inc=("cL", 16))
    P.emit("sync", lambda h: h.dma_start(out=cv_sb[:], in_=cv_d[:, :]), inc=("cL", 16))
    P.emit("sync", lambda h: h.dma_start(out=sinkb_sb[:], in_=sinkb_d[:, :]), inc=("cL", 16))
    cl = P.cnt["cL"]; gl = P.cnt["gL"]
    cdep = emit_conv_cast(P, "cv", [(wg, wg_bf, 2), (wo1, wo1_bf, 4)])
    aodep = emit_l1_att(P, q_d, kx1, vx1, ckx2, tw_sb, cv_sb, sinkb_sb, ao_all,
                        deps=[("cL", cl), ("gL", gl)])
    st = emit_l1_out(P, ao_all, wg_bf, wo1_bf, x, xo, deps=[aodep, cdep])
    for s_, v_ in st:
        P.wait("sync", s_, v_)
    P.finalize()
    return P


def prep_l1_attn(inp):
    cos, sin = rope_tables_l1()
    out = {}
    out["wqc"] = tile_lhsT(np.ascontiguousarray(inp["L1_attn_Wq_c"])).reshape(4, 128, 2048)
    out["wqu"] = tile_lhsT(np.ascontiguousarray(inp["L1_attn_Wq_up"])).reshape(NF, 128, 512)
    out["wk1"] = tile_lhsT(np.ascontiguousarray(inp["L1_attn_Wk"])).reshape(1, 128, 2048)
    out["wv1"] = tile_lhsT(np.ascontiguousarray(inp["L1_attn_Wv"])).reshape(1, 128, 2048)
    out["wcm"] = tile_lhsT(np.ascontiguousarray(inp["L1_attn_W_comp"])).reshape(1, 128, 256 * 128)
    wg = np.stack([tile_lhsT(np.ascontiguousarray(inp["L1_attn_Wg"][g]))
                   for g in range(4)])
    out["wg"] = np.ascontiguousarray(wg.reshape(64, 128, 512))
    out["wo1"] = tile_lhsT(np.ascontiguousarray(inp["L1_attn_Wo"])).reshape(NF, 128, 8192)
    out["qnw"] = inp["L1_attn_q_norm_w"].reshape(128, 1).astype(np.float32)
    out["knw"] = inp["L1_attn_k_norm_w"].reshape(128, 1).astype(np.float32)
    out["sinkb"] = np.ascontiguousarray(
        np.tile(inp["L1_attn_sink"].astype(np.float32)[None, :], (128, 1)))
    ct = np.zeros((2, 128, TOWN), np.float32)
    st_ = np.zeros((2, 128, TOWN), np.float32)
    for p in range(2):
        c_ = cos[_OWN[p]].T
        s_ = sin[_OWN[p]].T
        ct[p, 0:32] = c_; ct[p, 32:64] = c_
        st_[p, 0:32] = s_; st_[p, 32:64] = s_
    out["cos_p"] = [np.ascontiguousarray(ct[p]) for p in range(2)]
    out["sin_p"] = [np.ascontiguousarray(st_[p]) for p in range(2)]
    r_ = np.arange(128)[:, None]
    cw = np.arange(768)[None, :]
    tws = []
    for p in range(2):
        diff = (4 + p) * 128 + r_ - cw
        gen = ((diff >= 0) & (diff < 512)).astype(np.float32)
        t0 = gen.copy(); t0[:, 0:512] = 0.0
        t1 = gen.copy(); t1[:, 0:256] = 0.0
        tws.append(np.ascontiguousarray(np.concatenate([t0, t1, gen], axis=1)))
    out["tw_p"] = tws
    gb = np.arange(256)[None, :]
    cvs = []
    for p in range(2):
        m = np.zeros((128, NTO * 256), np.float32)
        for t in range(NTO):
            i = (2 * t + p) * 128 + r_
            m[:, t * 256:(t + 1) * 256] = (gb * 16 + 15 < i).astype(np.float32)
        cvs.append(np.ascontiguousarray(m))
    out["cv_p"] = cvs
    return out

def interleave_kx1(k0, k1):
    g = np.zeros((128, TFULL), k0.dtype)
    for t in range(NTO):
        g[:, (2 * t) * 128:(2 * t + 1) * 128] = k0[:, t * 128:(t + 1) * 128]
        g[:, (2 * t + 1) * 128:(2 * t + 2) * 128] = k1[:, t * 128:(t + 1) * 128]
    return g

def interleave_ck(c0, c1):
    g = np.zeros((256, c0.shape[1]), c0.dtype)
    for t in range(NTO):
        g[(2 * t) * 8:(2 * t) * 8 + 8] = c0[t * 8:t * 8 + 8]
        g[(2 * t + 1) * 8:(2 * t + 1) * 8 + 8] = c1[t * 8:t * 8 + 8]
    return np.ascontiguousarray(g.reshape(2, 128, c0.shape[1]))


_PROGS = {}

def get_prog(name):
    if name not in _PROGS:
        _PROGS[name] = {
            "ffn0": lambda: build_ffn_prog(True),
            "ffn1": lambda: build_ffn_prog(False),
            "l0proj": build_l0_proj,
            "l0p1": build_l0_p1,
            "l0p2": build_l0_p2,
            "l1proj": build_l1_proj,
            "l1att": build_l1_att,
        }[name]()
    return _PROGS[name]


def _runp(name, mk):
    P = get_prog(name)
    res = run_bass_kernel_spmd(P.nc, [mk(c) for c in range(NCORE)],
                               core_ids=list(range(NCORE)))
    return res.results


def _bfv(a):
    a = np.asarray(a)
    return a.view(BF) if (a.dtype.itemsize == 2 and a.dtype != BF) else a


def kernel(**inp):
    x0 = inp["L0_attn_hidden_states"]
    ones_d = np.ones((128, 128), np.float32)
    ident_d = np.eye(128, dtype=np.float32)
    pf0 = prep_ffn_inputs(inp, 0)
    pf1 = prep_ffn_inputs(inp, 1)
    pa0 = prep_l0_attn(inp)
    pa1 = prep_l1_attn(inp)
    xs = {(b, p): x_to_fm(x0[b], p) for b in range(4) for p in range(2)}

    def ffn(layer):
        pf = pf0 if layer == 0 else pf1
        r = _runp("ffn0" if layer == 0 else "ffn1",
                  lambda c: dict(pf, x=xs[(c // 2, c % 2)]))
        for c in range(NCORE):
            xs[(c // 2, c % 2)] = np.asarray(r[c]["xo"])

    def l0_attn():
        r = _runp("l0proj", lambda c: dict(
            wq=pa0["wq"], wk=pa0["wk"], wv=pa0["wv"], ones_d=ones_d,
            ident_d=ident_d, cosd=pa0["cos_p"][c % 2], sind=pa0["sin_p"][c % 2],
            x=xs[(c // 2, c % 2)]))
        q = {c: _bfv(r[c]["q"]) for c in range(NCORE)}
        kxg = {b: interleave_kx(_bfv(r[2 * b]["kx"]), _bfv(r[2 * b + 1]["kx"]))
               for b in range(4)}
        vxg = {b: interleave_kx(_bfv(r[2 * b]["vx"]), _bfv(r[2 * b + 1]["vx"]))
               for b in range(4)}
        r = _runp("l0p1", lambda c: dict(
            q=q[c], kxg=kxg[c // 2], mt=pa0["mt_p"][c % 2],
            ones_d=ones_d, ident_d=ident_d))
        ts = {c: np.asarray(r[c]["ts"]) for c in range(NCORE)}
        r = _runp("l0p2", lambda c: dict(
            q=q[c], kxg=kxg[c // 2], vxg=vxg[c // 2],
            ts_s=ts[c], ts_p=ts[c ^ 1],
            mt=pa0["mt_p"][c % 2], tb=pa0["tb_p"][c % 2],
            blk8_d=pa0["blk8_d"], bc16_d=pa0["bc16_d"],
            onesrow_d=pa0["onesrow_d"], ident_d=ident_d,
            wo=pa0["wo"], x=xs[(c // 2, c % 2)]))
        for c in range(NCORE):
            xs[(c // 2, c % 2)] = np.asarray(r[c]["xo"])

    def l1_attn():
        r = _runp("l1proj", lambda c: dict(
            wqc=pa1["wqc"], wqu=pa1["wqu"], wk1=pa1["wk1"], wv1=pa1["wv1"],
            wcm=pa1["wcm"], qnw=pa1["qnw"], knw=pa1["knw"],
            cosd=pa1["cos_p"][c % 2], sind=pa1["sin_p"][c % 2],
            ones_d=ones_d, ident_d=ident_d, x=xs[(c // 2, c % 2)]))
        q = {c: _bfv(r[c]["q"]) for c in range(NCORE)}
        kx1 = {b: interleave_kx1(_bfv(r[2 * b]["kx"]).reshape(128, TOWN),
                                 _bfv(r[2 * b + 1]["kx"]).reshape(128, TOWN))
               for b in range(4)}
        vx1 = {b: interleave_kx1(_bfv(r[2 * b]["vx"]).reshape(128, TOWN),
                                 _bfv(r[2 * b + 1]["vx"]).reshape(128, TOWN))
               for b in range(4)}
        ck2 = {b: interleave_ck(_bfv(r[2 * b]["ckx"]).reshape(128, 128),
                                _bfv(r[2 * b + 1]["ckx"]).reshape(128, 128))
               for b in range(4)}
        r = _runp("l1att", lambda c: dict(
            q=q[c], kx1=kx1[c // 2], vx1=vx1[c // 2], ckx2=ck2[c // 2],
            tw=pa1["tw_p"][c % 2], cv=pa1["cv_p"][c % 2], sinkb=pa1["sinkb"],
            ident_d=ident_d, wg=pa1["wg"], wo1=pa1["wo1"],
            x=xs[(c // 2, c % 2)]))
        for c in range(NCORE):
            xs[(c // 2, c % 2)] = np.asarray(r[c]["xo"])

    ffn(0)
    l0_attn()
    ffn(0)
    ffn(1)
    l1_attn()
    ffn(1)

    out = np.zeros((4, TFULL, DM), np.float32)
    for b in range(4):
        for p in range(2):
            out[b][_OWN[p]] = xs[(b, p)].T
    return out
